# version 01 (final)
# baseline (speedup 1.0000x reference)
import numpy as np
import ml_dtypes
from contextlib import ExitStack

import concourse.bass as bass
import concourse.mybir as mybir
from concourse.bass_utils import run_bass_kernel_spmd
from concourse.bass import IndirectOffsetOnAxis

F32 = mybir.dt.float32
BF16 = mybir.dt.bfloat16
I32 = mybir.dt.int32
AF = mybir.ActivationFunctionType
ALU = mybir.AluOpType
AX = mybir.AxisListType

B, T, D = 4, 8192, 1024
NCORE = 8
TPC = 4096
NTILE = TPC // 128
NSLAB = T // 512
CAP = 5120
HID = 4096
ATT_EPS = 1e-6
LN_EPS = 1e-5
INFO_ROWS = 8 * CAP + 64
OUT_ROWS = CAP + 1
XF_SH = TPC + 1


def _rope_tables():
    half = 32
    inv = 1.0 / (10000.0 ** (np.arange(half, dtype=np.float64) / half))
    ang = np.arange(T, dtype=np.float64)[:, None] * inv[None, :]
    sin, cos = np.sin(ang), np.cos(ang)
    cos64 = np.concatenate([cos, cos], axis=1)
    sin64 = np.concatenate([-sin, sin], axis=1)
    cosrep = np.tile(cos64, (1, 8)).astype(ml_dtypes.bfloat16)
    sinrep = np.tile(sin64, (1, 8)).astype(ml_dtypes.bfloat16)
    return np.ascontiguousarray(cosrep), np.ascontiguousarray(sinrep)


class Sem:
    def __init__(self, h):
        self.h = h
        self.n = 0


class Prog:
    ENGS = ("sync", "gpsimd", "vector", "scalar", "tensor")

    def __init__(self):
        self.q = {e: [] for e in self.ENGS}

    def on(self, eng, fn):
        self.q[eng].append(fn)


def build_program(debug=(), phases=4):
    nc = bass.Bass("TRN2", target_bir_lowering=False)
    dbg = set(debug)

    def inp(name, shape, dt=F32):
        return nc.dram_tensor(name, shape, dt, kind="ExternalInput").ap()

    x_b = inp("x_b", [T, D])
    wq = inp("wq", [D, 512]); wk = inp("wk", [D, 512])
    wv = inp("wv", [D, 520]); wg = inp("wg", [D, 512])
    wo = inp("wo", [D, D])
    biases = inp("biases", [1, 3088])
    bg_col = inp("bg_col", [128, 4])
    n1s = inp("n1s", [128, 8])
    n2s = inp("n2s", [128, 8])
    wr = inp("wr", [D, 8])
    w1e = inp("w1e", [D, HID]); b1e = inp("b1e", [128, 32])
    w2e = inp("w2e", [HID, D]); b2e = inp("b2e", [1, D])
    fw1 = inp("fw1", [HID, D])
    fcw = inp("fcw", [128, 32, 3]); fcb = inp("fcb", [128, 32])
    fw2 = inp("fw2", [D, HID])
    cmask = inp("cmask", [8, 1])
    tokoff = inp("tokoff", [128, 1])
    halosel = inp("halosel", [16, 2])
    woidx = inp("woidx", [128, 8], I32)
    x_my = inp("x_my", [TPC, D])
    eidx = inp("eidx", [128, 1], I32)

    cosrep, sinrep = _rope_tables()
    c_cos = nc.inline_tensor(cosrep, name="c_cos").ap()
    c_sin = nc.inline_tensor(sinrep, name="c_sin").ap()
    c_iota = nc.inline_tensor(
        np.arange(128, dtype=np.float32).reshape(128, 1), name="c_iota").ap()
    c_onesrow = nc.inline_tensor(
        np.ones((1, 128), dtype=ml_dtypes.bfloat16), name="c_onesrow").ap()
    pairsel = np.zeros((2, 128), dtype=ml_dtypes.bfloat16)
    pairsel[0, 0:64] = 1.0
    pairsel[1, 64:128] = 1.0
    c_pairsel = nc.inline_tensor(pairsel, name="c_pairsel").ap()
    trimat = np.triu(np.ones((128, 128), dtype=np.float32), 1)
    c_tri = nc.inline_tensor(trimat.astype(ml_dtypes.bfloat16), name="c_tri").ap()
    c_tric = nc.inline_tensor(
        (1.0 - trimat).astype(ml_dtypes.bfloat16), name="c_tric").ap()
    c_iota8 = nc.inline_tensor(
        np.ascontiguousarray(np.tile(np.arange(8, dtype=np.float32), (128, 1))),
        name="c_iota8").ap()
    c_ones8 = nc.inline_tensor(
        np.ones((8, 128), dtype=ml_dtypes.bfloat16), name="c_ones8").ap()
    c_ones8f = nc.inline_tensor(
        np.ones((8, 128), dtype=np.float32), name="c_ones8f").ap()
    c_onescol = nc.inline_tensor(
        np.ones((128, 1), dtype=ml_dtypes.bfloat16), name="c_onescol").ap()
    c_onescolf = nc.inline_tensor(
        np.ones((128, 1), dtype=np.float32), name="c_onescolf").ap()
    c_onesrowf = nc.inline_tensor(
        np.ones((1, 128), dtype=np.float32), name="c_onesrowf").ap()
    c_ident = nc.inline_tensor(
        np.eye(128, dtype=np.float32), name="c_ident").ap()

    y_out = nc.dram_tensor("y", [TPC, D], F32, kind="ExternalOutput").ap()
    dbg_out = {}
    def dbgout(name, shape, dt=F32):
        if name in dbg:
            dbg_out[name] = nc.dram_tensor(
                "dbg_" + name, shape, dt, kind="ExternalOutput").ap()
        return dbg_out.get(name)

    xn_hbm = nc.dram_tensor("xn_hbm", [T, D], BF16).ap()
    q_hbm = nc.dram_tensor("q_hbm", [T, 512], BF16).ap()
    k_hbm = nc.dram_tensor("k_hbm", [T, 512], BF16).ap()
    ag_in = nc.dram_tensor("ag_in", [512, T], BF16).ap()
    ag_out = nc.dram_tensor("ag_out", [NCORE * 512, T], BF16).ap()
    delta_hbm = nc.dram_tensor("delta_hbm", [TPC, D], BF16).ap()
    xf_in = nc.dram_tensor("xf_in", [XF_SH, D], BF16).ap()
    xf_all = nc.dram_tensor("xf_all", [NCORE * XF_SH, D], BF16).ap()
    cnt_in = nc.dram_tensor("cnt_in", [1, 256], F32).ap()
    cnt_all = nc.dram_tensor("cnt_all", [8, 256], F32).ap()
    info_in = nc.dram_tensor("info_in", [INFO_ROWS, 2], F32).ap()
    info_all = nc.dram_tensor("info_all", [INFO_ROWS, 2], F32).ap()
    stage_hbm = nc.dram_tensor("stage_hbm", [CAP, D], BF16).ap()
    ob_in = nc.dram_tensor("ob_in", [OUT_ROWS, D], BF16).ap()
    ob_all = nc.dram_tensor("ob_all", [NCORE * OUT_ROWS, D], BF16).ap()
    w1t_hbm = nc.dram_tensor("w1t_hbm", [HID, D], BF16).ap()
    w2t_hbm = nc.dram_tensor("w2t_hbm", [D, HID], BF16).ap()
    xq_hbm = nc.dram_tensor("xq_hbm", [TPC + 32, D], BF16).ap()
    x1_hbm = nc.dram_tensor("x1_hbm", [TPC, D], BF16).ap()
    halo_in = nc.dram_tensor("halo_in", [2, D], BF16).ap()
    halo_all = nc.dram_tensor("halo_all", [16, D], BF16).ap()

    d_delta = dbgout("delta", [TPC, D])

    RG = [list(range(8))]

    P = Prog()
    with ExitStack() as ctx:
        def sb(name, shape, dt):
            return ctx.enter_context(nc.sbuf_tensor(name, shape, dt))

        def psum(name, shape, dt=F32):
            return ctx.enter_context(nc.psum_tensor(name, shape, dt))

        def sem(name):
            return Sem(ctx.enter_context(nc.semaphore(name)))

        WAR = sb("WAR", [128, 65536], BF16)
        CAR = sb("CAR", [128, 24576], BF16)
        FAR = sb("FAR", [128, 3072], F32)
        _off = [0]
        def vw(n, d0=None):
            a = _off[0]; _off[0] += n
            ap = WAR[:, a:a + n]
            if d0 is not None:
                ap = ap.rearrange("p (c d) -> p c d", c=d0)
            return ap
        wq_sb = vw(4096, 8)
        wk_sb = vw(4096, 8)
        wv_sb = vw(4160, 8)
        wg_sb = vw(4096, 8)
        wo_sb = vw(8192, 8)
        xnd_sb = vw(4096, 8)
        qk_sb = vw(4096, 4)
        r1_sb = vw(4096, 4)
        r2_sb = vw(4096, 4)
        v_sb = vw(2080, 4)
        g_sb = vw(2048, 4)
        qd_sb = vw(2048, 4)
        kd_sb = vw(2048, 4)
        qdo_sb = vw(2048, 4)[0:64]
        kdo_sb = vw(2048, 4)[0:64]
        cos_sb = vw(2048, 4)
        sin_sb = vw(2048, 4)
        numd_sb = vw(2048, 4)
        outd_sb = vw(2048, 4)
        assert _off[0] <= 65536, _off[0]
        w1e_sb = WAR[:, 0:32768].rearrange("p (c d) -> p c d", c=8)
        w2e_sb = WAR[:, 32768:65536].rearrange("p (c d) -> p c d", c=32)
        fw1t_sb = WAR[:, 0:32768].rearrange("p (c d) -> p c d", c=8)
        fw2t_sb = WAR[:, 32768:65536].rearrange("p (c d) -> p c d", c=32)
        dtile_sb = CAR[:, 0:2048].rearrange("p (c d) -> p c d", c=2)
        xn_tile = CAR[:, 2048:4096].rearrange("p (c d) -> p c d", c=2)
        xf_tile = CAR[:, 2048:4096].rearrange("p (c d) -> p c d", c=2)
        ohb_sb = CAR[:, 4096:4112].rearrange("p (c d) -> p c d", c=2)
        bias_sb = CAR[0:1, 4128:7216]
        hgel_sb = CAR[:, 0:16384].rearrange("p (c d) -> p c d", c=32)
        gbuf_sb = CAR[:, 16384:20480].rearrange("p (c d) -> p c d", c=4)
        orow_sb = CAR[:, 16384:17408]
        xq3_sb = CAR[:, 20480:24576].rearrange("p (c d) -> p c d", c=8)
        h_sb = CAR[:, 0:16512].rearrange("p (c d) -> p c d", c=32)
        xqf_sb = CAR[:, 16512:20864].rearrange("p (c d) -> p c d", c=8)
        x1t_sb = CAR[:, 20864:21888]
        p4a_sb = CAR[:, 0:5120].rearrange("p (c d) -> p c d", c=5)
        x_tile = FAR[:, 0:2048].rearrange("p (c d) -> p c d", c=2)
        scr32 = FAR[:, 2048:3072]
        y32 = FAR[:, 1024:2048]

        bgc_sb = sb("bgc_sb", [128, 4], F32)
        n1_sb = sb("n1_sb", [128, 8], F32)
        n2_sb = sb("n2_sb", [128, 8], F32)
        iota_sb = sb("iota_sb", [128, 1], F32)
        onesrow_sb = sb("onesrow_sb", [1, 128], BF16)
        onesrowf_sb = sb("onesrowf_sb", [1, 128], F32)
        pairsel_sb = sb("pairsel_sb", [2, 128], BF16)
        tri_sb = sb("tri_sb", [128, 128], BF16)
        tric_sb = sb("tric_sb", [128, 128], BF16)
        iota8_sb = sb("iota8_sb", [128, 8], F32)
        cmask_sb = sb("cmask_sb", [8, 1], F32)
        tokoff_sb = sb("tokoff_sb", [128, 1], F32)
        halosel_sb = sb("halosel_sb", [16, 2], BF16)
        woidx_sb = sb("woidx_sb", [128, 8], I32)
        zrow_sb = sb("zrow_sb", [1, 512], BF16)
        zbig_sb = sb("zbig_sb", [128, 512], BF16)
        stat_sb = sb("stat_sb", [128, 16], F32)
        at_sb = sb("at_sb", [128, 128], BF16)
        s_sb = sb("s_sb", [64, 8, 65], BF16)
        denb_fl = CAR[0:1, 16384:24576]
        route_sb = sb("route_sb", [128, 32, 8], F32)
        prob_sb = sb("prob_sb", [128, 8], F32)
        rsc_sb = sb("rsc_sb", [128, 12], F32)
        wrf_sb = sb("wrf_sb", [128, 8, 8], BF16)
        cnt_sb = sb("cnt_sb", [8, 256], F32)
        cntb_sb = sb("cntb_sb", [8, 8], BF16)
        ones8_sb = sb("ones8_sb", [8, 128], BF16)
        ones8f_sb = sb("ones8f_sb", [8, 128], F32)
        gi_sb = sb("gi_sb", [128, 32, 2], I32)
        vg_sb = sb("vg_sb", [128, 2, 2], F32)
        sidx_sb = sb("sidx_sb", [128, 2], I32)
        mki_sb = sb("mki_sb", [128, 1], I32)
        dumpc_sb = sb("dumpc_sb", [128, 4], F32)
        info_sb = sb("info_sb", [128, 40, 2], F32)
        gidx_sb = sb("gidx_sb", [128, 40], I32)
        iscr_sb = sb("iscr_sb", [128, 40, 4], F32)
        b1e_sb = sb("b1e_sb", [128, 32], F32)
        b2e_sb = sb("b2e_sb", [1, 1024], BF16)
        fcw_sb = sb("fcw_sb", [128, 32, 3], F32)
        fcb_sb = sb("fcb_sb", [128, 32], F32)
        rcp1_sb = sb("rcp1_sb", [128, 32], F32)
        wsc_sb = sb("wsc_sb", [128, 4], F32)
        wacc_sb = sb("wacc_sb", [128, 4], F32)
        hq_i32 = sb("hq_i32", [128, 512], I32)
        rc2_sb = sb("rc2_sb", [128, 8], F32)
        srow_sb = sb("srow_sb", [1, 520], BF16)
        eidx_sb = sb("eidx_sb", [128, 1], I32)
        onescol_sb = sb("onescol_sb", [128, 1], BF16)
        onescolf_sb = sb("onescolf_sb", [128, 1], F32)
        ident_sb = sb("ident_sb", [128, 128], F32)
        prob2_sb = sb("prob2_sb", [128, 32], F32)
        tot_sb = sb("tot_sb", [8, 8], F32)
        wsum_sb = sb("wsum_sb", [1, 4], F32)
        ternb_sb = sb("ternb_sb", [128, 512], BF16)
        s127_sb = sb("s127_sb", [128, 4], F32)
        halo16_sb = CAR[0:16, 22400:23424]
        hrow_sb = CAR[0:2, 23424:24448]
        m32_sb = FAR[:, 512:1024]
        cvt_sb = FAR[:, 0:512]

        pb = [psum("pb%d" % i, [128, 512], F32) for i in range(8)]

        s_ld = sem("s_ld"); s_wld = sem("s_wld"); s_xld = sem("s_xld")
        s_csl = sem("s_csl"); s_xnw = sem("s_xnw"); s_trx = sem("s_trx")
        s_qkw = sem("s_qkw"); s_trd = sem("s_trd"); s_mm = sem("s_mm")
        s_ev = sem("s_ev"); s_sc = sem("s_sc"); s_gp = sem("s_gp")
        s_odw = sem("s_odw"); s_cc = sem("s_cc"); s_wol = sem("s_wol")
        s_dlw = sem("s_dlw"); s_fin = sem("s_fin"); s_rt = sem("s_rt")
        s_tld = sem("s_tld")

        for dst, src in ((iota_sb[:, :], c_iota), (onesrow_sb[:, :], c_onesrow),
                         (pairsel_sb[:, :], c_pairsel), (n1_sb[:, :], n1s),
                         (n2_sb[:, :], n2s), (tri_sb[:, :], c_tri),
                         (tric_sb[:, :], c_tric), (iota8_sb[:, :], c_iota8),
                         (cmask_sb[:, :], cmask), (tokoff_sb[:, :], tokoff),
                         (bgc_sb[:, :], bg_col), (woidx_sb[:, :], woidx),
                         (ident_sb[:, :], c_ident), (onescolf_sb[:, :], c_onescolf)):
            def ldc(e, d=dst, s=src):
                e.dma_start(out=d, in_=s).then_inc(s_ld.h, 16)
            P.on("sync", ldc); s_ld.n += 16
        for ch in range(8):
            for wsb_, wsrc in ((wq_sb, wq), (wk_sb, wk), (wv_sb, wv),
                               (wg_sb, wg), (wo_sb, wo)):
                def wl(e, d=wsb_[:, ch, :], s=wsrc[128 * ch:128 * (ch + 1), :]):
                    e.dma_start(out=d, in_=s).then_inc(s_wld.h, 16)
                P.on("gpsimd", wl); s_wld.n += 16
        def wl2(e):
            e.dma_start(out=bias_sb[:, :], in_=biases).then_inc(s_wld.h, 16)
        P.on("gpsimd", wl2); s_wld.n += 16
        def wl3(e):
            e.dma_start(out=halosel_sb[:, :], in_=halosel).then_inc(s_wld.h, 16)
        P.on("gpsimd", wl3); s_wld.n += 16

        wldn, ldn = s_wld.n, s_ld.n
        def fold(e):
            e.memset(zrow_sb[:, :], 0.0)
            e.memset(zbig_sb[:, :], 0.0)
            e.memset(at_sb[:, :], 0.0)
            e.memset(s_sb[:, :, :], 0.0)
            e.memset(iscr_sb[:, :, :].rearrange("p c d -> p (c d)"), 0.0)
            for ch in range(8):
                for wsb_, wdt in ((wq_sb, 512), (wk_sb, 512), (wv_sb, 520),
                                  (wg_sb, 512)):
                    m = e.tensor_scalar(out=wsb_[:, ch, 0:wdt],
                                        in0=wsb_[:, ch, 0:wdt],
                                        scalar1=n1_sb[:, ch:ch + 1], scalar2=None,
                                        op0=ALU.mult)
            m.then_inc(s_ev.h, 1)
        def fold_go(e, a=wldn, b=ldn):
            e.wait_ge(s_wld.h, a)
            e.wait_ge(s_ld.h, b)
            fold(e)
        P.on("vector", fold_go); s_ev.n += 1
        EV_FOLD = s_ev.n

        norm_ev = {}
        rt_tgt = {}
        for t in range(T // 128):
            rt_tgt[t] = s_rt.n + 1
            def xload(e, t=t, w=max(0, 16 * (t - 1))):
                if t >= 2:
                    e.wait_ge(s_xnw.h, w)
                e.dma_start(out=x_tile[:, t % 2, :],
                            in_=x_b[128 * t:128 * (t + 1), :]).then_inc(s_xld.h, 16)
            P.on("sync", xload); s_xld.n += 16
            o3 = (t % 2) * 3
            def stats(e, t=t, w=s_xld.n, o3=o3,
                      wev=(norm_ev.get(t - 2, 0))):
                e.wait_ge(s_xld.h, w)
                if t >= 2:
                    e.wait_ge(s_ev.h, wev)
                e.activation(out=scr32[:, :], in_=x_tile[:, t % 2, :],
                             func=AF.Square, accum_out=stat_sb[:, o3:o3 + 1])
                e.activation(out=stat_sb[:, o3 + 1:o3 + 2], in_=stat_sb[:, o3:o3 + 1],
                             func=AF.Sqrt, scale=1.0 / D).then_inc(s_sc.h, 1)
            P.on("scalar", stats); s_sc.n += 1
            def norm(e, t=t, w=s_sc.n, wf=EV_FOLD, wxw=max(0, 16 * (t - 1)), o3=o3):
                e.wait_ge(s_sc.h, w)
                e.wait_ge(s_ev.h, wf)
                if t >= 2:
                    e.wait_ge(s_xnw.h, wxw)
                e.reciprocal(out=stat_sb[:, o3 + 2:o3 + 3],
                             in_=stat_sb[:, o3 + 1:o3 + 2]).then_inc(s_rt.h, 1)
                e.wait_ge(s_rt.h, rt_tgt[t])
                e.tensor_scalar(out=xn_tile[:, t % 2, :], in0=x_tile[:, t % 2, :],
                                scalar1=stat_sb[:, o3 + 2:o3 + 3], scalar2=None,
                                op0=ALU.mult).then_inc(s_ev.h, 1)
            s_rt.n += 1
            P.on("vector", norm); s_ev.n += 1
            norm_ev[t] = s_ev.n
            def xnwr(e, t=t, w=s_ev.n):
                e.wait_ge(s_ev.h, w)
                e.dma_start(out=xn_hbm[128 * t:128 * (t + 1), :],
                            in_=xn_tile[:, t % 2, :]).then_inc(s_xnw.h, 16)
            P.on("gpsimd", xnwr); s_xnw.n += 16

        for s in range(NSLAB):
            wgp = s_gp.n
            for j in range(4):
                def lcs(e, s=s, j=j, w=wgp):
                    if s >= 1 and j == 0:
                        e.wait_ge(s_gp.h, w)
                    e.dma_start(out=cos_sb[:, j, :],
                                in_=c_cos[512 * s + 128 * j:512 * s + 128 * (j + 1), :]) \
                        .then_inc(s_csl.h, 16)
                    e.dma_start(out=sin_sb[:, j, :],
                                in_=c_sin[512 * s + 128 * j:512 * s + 128 * (j + 1), :]) \
                        .then_inc(s_csl.h, 16)
                P.on("gpsimd", lcs); s_csl.n += 32
            for ch in range(8):
                def trp(e, s=s, ch=ch, wxn=16 * (4 * s + 4), wmm=s_mm.n):
                    e.wait_ge(s_xnw.h, wxn)
                    if s >= 1 and ch == 0:
                        e.wait_ge(s_mm.h, wmm)
                    e.dma_start_transpose(
                        xnd_sb[:, ch, :],
                        xn_hbm[512 * s:512 * (s + 1), 128 * ch:128 * (ch + 1)]) \
                        .then_inc(s_trx.h, 16)
                P.on("sync", trp); s_trx.n += 16

            for j in range(4):
                def pmm(e, s=s, j=j, wtr=s_trx.n, wwl=s_wld.n,
                        wsc=s_sc.n, wev=s_ev.n, wqk=s_qkw.n):
                    e.wait_ge(s_trx.h, wtr)
                    if s == 0 and j == 0:
                        e.wait_ge(s_wld.h, wwl)
                        e.wait_ge(s_ev.h, EV_FOLD)
                    if s + j > 0:
                        e.wait_ge(s_sc.h, wsc)
                        e.wait_ge(s_ev.h, wev)
                        e.wait_ge(s_qkw.h, wqk)
                    for (wsb_, boff, bank) in ((wq_sb, 0, 0), (wk_sb, 512, 1)):
                        for ch in range(8):
                            e.matmul(pb[bank][:, :],
                                     xnd_sb[:, ch, 128 * j:128 * (j + 1)],
                                     wsb_[:, ch, :], start=(ch == 0), stop=False)
                        e.matmul(pb[bank][:, :], onesrow_sb[:, :],
                                 bias_sb[:, boff:boff + 512],
                                 start=False, stop=True)
                    for ch in range(8):
                        e.matmul(pb[2][:, 0:256],
                                 xnd_sb[:, ch, 128 * j:128 * (j + 1)],
                                 wv_sb[:, ch, 0:256], start=(ch == 0), stop=False)
                    e.matmul(pb[2][:, 0:256], onesrow_sb[:, :],
                             bias_sb[:, 1024:1280], start=False, stop=True)
                    for ch in range(8):
                        e.matmul(pb[3][:, 0:264],
                                 xnd_sb[:, ch, 128 * j:128 * (j + 1)],
                                 wv_sb[:, ch, 256:520], start=(ch == 0), stop=False)
                    e.matmul(pb[3][:, 0:264], onesrow_sb[:, :],
                             bias_sb[:, 1280:1544], start=False, stop=True) \
                        .then_inc(s_mm.h, 1)
                P.on("tensor", pmm); s_mm.n += 1
                def pev(e, s=s, j=j, wmm=s_mm.n):
                    e.wait_ge(s_mm.h, wmm)
                    e.copy(out=qk_sb[:, j, 0:512], in_=pb[0][:, :])
                    e.copy(out=qk_sb[:, j, 512:1024], in_=pb[1][:, :])
                    e.copy(out=v_sb[:, j, 0:256], in_=pb[2][:, 0:256])
                    e.copy(out=v_sb[:, j, 256:520], in_=pb[3][:, 0:264]) \
                        .then_inc(s_sc.h, 1)
                P.on("scalar", pev); s_sc.n += 1
            for chq in range(4):
                def gmm(e, s=s, chq=chq, wsc=s_sc.n, wgp=s_gp.n):
                    e.wait_ge(s_sc.h, wsc)
                    if chq == 0:
                        e.wait_ge(s_gp.h, wgp)
                    for ch in range(8):
                        m = e.matmul(pb[2][:, :],
                                     wg_sb[:, ch, 128 * chq:128 * (chq + 1)],
                                     xnd_sb[:, ch, :], start=(ch == 0),
                                     stop=(ch == 7))
                    m.then_inc(s_mm.h, 1)
                P.on("tensor", gmm); s_mm.n += 1
                def gev(e, s=s, chq=chq, wmm=s_mm.n):
                    e.wait_ge(s_mm.h, wmm)
                    e.activation(out=g_sb[:, chq, :], in_=pb[2][:, :],
                                 func=AF.Sigmoid, bias=bgc_sb[:, chq:chq + 1],
                                 scale=1.0).then_inc(s_sc.h, 1)
                P.on("scalar", gev); s_sc.n += 1

            def rope(e, s=s, wcs=s_csl.n, wsc=s_sc.n):
                e.wait_ge(s_csl.h, wcs)
                e.wait_ge(s_sc.h, wsc)
                for j in range(4):
                    for hb in range(2):
                        o = 512 * hb
                        e.tensor_mul(r1_sb[:, j, o:o + 512], qk_sb[:, j, o:o + 512],
                                     cos_sb[:, j, :])
                        for h8 in range(8):
                            b0 = o + 64 * h8
                            e.tensor_mul(r2_sb[:, j, b0:b0 + 32],
                                         qk_sb[:, j, b0 + 32:b0 + 64],
                                         sin_sb[:, j, 64 * h8:64 * h8 + 32])
                            e.tensor_mul(r2_sb[:, j, b0 + 32:b0 + 64],
                                         qk_sb[:, j, b0:b0 + 32],
                                         sin_sb[:, j, 64 * h8 + 32:64 * h8 + 64])
                        e.tensor_add(qk_sb[:, j, o:o + 512], r1_sb[:, j, o:o + 512],
                                     r2_sb[:, j, o:o + 512])
                for j in range(4):
                    m = e.tensor_scalar(out=r1_sb[:, j, :], in0=qk_sb[:, j, :],
                                        scalar1=0.0, scalar2=None, op0=ALU.min)
                m.then_inc(s_ev.h, 1)
            P.on("vector", rope); s_ev.n += 1
            def phiexp(e, s=s, wev=s_ev.n):
                e.wait_ge(s_ev.h, wev)
                for j in range(4):
                    m = e.activation(out=r2_sb[:, j, :], in_=r1_sb[:, j, :],
                                     func=AF.Exp)
                m.then_inc(s_sc.h, 1)
            P.on("scalar", phiexp); s_sc.n += 1
            def phifin(e, s=s, wsc=s_sc.n):
                e.wait_ge(s_sc.h, wsc)
                for j in range(4):
                    m = e.scalar_tensor_tensor(out=qk_sb[:, j, :], in0=qk_sb[:, j, :],
                                               scalar=0.0, in1=r2_sb[:, j, :],
                                               op0=ALU.max, op1=ALU.add)
                m.then_inc(s_gp.h, 1)
            P.on("vector", phifin); s_gp.n += 1

            for j in range(4):
                def qkwr(e, s=s, j=j, w=s_gp.n):
                    e.wait_ge(s_gp.h, w)
                    r0 = 512 * s + 128 * j
                    e.dma_start(out=q_hbm[r0:r0 + 128, :],
                                in_=qk_sb[:, j, 0:512]).then_inc(s_qkw.h, 16)
                    e.dma_start(out=k_hbm[r0:r0 + 128, :],
                                in_=qk_sb[:, j, 512:1024]).then_inc(s_qkw.h, 16)
                P.on("gpsimd", qkwr); s_qkw.n += 32
            for ch4 in range(4):
                def trqk(e, s=s, ch4=ch4, w=s_qkw.n, wev=s_ev.n):
                    e.wait_ge(s_qkw.h, w)
                    if s >= 1 and ch4 == 0:
                        e.wait_ge(s_ev.h, wev)
                    e.dma_start_transpose(
                        qd_sb[:, ch4, :],
                        q_hbm[512 * s:512 * (s + 1), 128 * ch4:128 * (ch4 + 1)]) \
                        .then_inc(s_trd.h, 16)
                    e.dma_start_transpose(
                        kd_sb[:, ch4, :],
                        k_hbm[512 * s:512 * (s + 1), 128 * ch4:128 * (ch4 + 1)]) \
                        .then_inc(s_trd.h, 16)
                P.on("sync", trqk); s_trd.n += 32
            def oddcp(e, s=s, w=s_trd.n):
                e.wait_ge(s_trd.h, w)
                for ch4 in range(4):
                    e.tensor_copy(out=qdo_sb[:, ch4, :], in_=qd_sb[64:128, ch4, :])
                    m = e.tensor_copy(out=kdo_sb[:, ch4, :], in_=kd_sb[64:128, ch4, :])
                m.then_inc(s_ev.h, 1)
            P.on("vector", oddcp); s_ev.n += 1

            for j in range(4):
                for h in range(8):
                    ch4, odd = h // 2, h % 2
                    qdh = (qdo_sb[:, ch4, 128 * j:128 * (j + 1)] if odd
                           else qd_sb[0:64, ch4, 128 * j:128 * (j + 1)])
                    kdh = (kdo_sb[:, ch4, 128 * j:128 * (j + 1)] if odd
                           else kd_sb[0:64, ch4, 128 * j:128 * (j + 1)])
                    grp, pos = h // 4, h % 4
                    spsl = pb[6 + grp][0:64, 65 * pos:65 * pos + 65]
                    sfirst = (s == 0 and j == 0 and pos == 0)
                    slast = (s == NSLAB - 1 and j == 3 and pos == 3)
                    def mm1(e, kdh=kdh, qdh=qdh, wev=s_ev.n, wsc=s_sc.n):
                        e.wait_ge(s_ev.h, wev)
                        e.wait_ge(s_sc.h, wsc)
                        e.matmul(pb[4][0:128, 0:128], kdh, qdh,
                                 start=True, stop=True).then_inc(s_mm.h, 1)
                    P.on("tensor", mm1); s_mm.n += 1
                    def at_ev(e, wmm=s_mm.n):
                        e.wait_ge(s_mm.h, wmm)
                        e.copy(out=at_sb[0:64, :], in_=pb[4][0:64, 0:128])
                        e.copy(out=at_sb[64:128, 64:128],
                               in_=pb[4][64:128, 64:128]).then_inc(s_sc.h, 1)
                    P.on("scalar", at_ev); s_sc.n += 1
                    def mm234(e, j=j, h=h, qdh=qdh, spsl=spsl, sfirst=sfirst,
                              slast=slast, wsc=s_sc.n, wev=s_ev.n):
                        e.wait_ge(s_sc.h, wsc)
                        e.wait_ge(s_ev.h, wev)
                        e.matmul(pb[0][0:65, 0:128], v_sb[:, j, 65 * h:65 * h + 65],
                                 at_sb[:, :], start=True, stop=False)
                        e.matmul(pb[0][0:65, 0:128], s_sb[:, h, :], qdh,
                                 start=False, stop=True)
                        e.matmul(spsl, qk_sb[:, j, 512 + 64 * h:512 + 64 * h + 64],
                                 v_sb[:, j, 65 * h:65 * h + 65],
                                 start=sfirst, stop=slast).then_inc(s_mm.h, 1)
                    P.on("tensor", mm234); s_mm.n += 1
                    def scan_ev(e, j=j, h=h, spsl=spsl, wmm=s_mm.n):
                        e.wait_ge(s_mm.h, wmm)
                        ch4b, oddb = h // 2, h % 2
                        e.tensor_copy(out=numd_sb[64 * oddb:64 * oddb + 64, ch4b,
                                                  128 * j:128 * (j + 1)],
                                      in_=pb[0][0:64, 0:128])
                        e.tensor_copy(out=denb_fl[:, 128 * (4 * h + j):
                                                  128 * (4 * h + j) + 128],
                                      in_=pb[0][64:65, 0:128])
                        e.tensor_copy(out=s_sb[:, h, :], in_=spsl) \
                            .then_inc(s_ev.h, 1)
                    P.on("vector", scan_ev); s_ev.n += 1

            def gat_v(e, s=s, wev=s_ev.n):
                e.wait_ge(s_ev.h, wev)
                e.tensor_scalar(out=denb_fl[:, :], in0=denb_fl[:, :],
                                scalar1=ATT_EPS, scalar2=None,
                                op0=ALU.add).then_inc(s_ev.h, 1)
            P.on("vector", gat_v); s_ev.n += 1
            for c4 in range(4):
                def gat_mm(e, s=s, c4=c4, wev=s_ev.n, wga=s_gp.n):
                    e.wait_ge(s_ev.h, wev)
                    if s + c4 > 0:
                        e.wait_ge(s_gp.h, wga)
                    e.matmul(pb[5][0:64, :], onesrow_sb[0:1, 0:64],
                             denb_fl[:, 512 * (2 * c4):512 * (2 * c4) + 512],
                             start=True, stop=True)
                    e.matmul(pb[5][64:128, :], onesrow_sb[0:1, 0:64],
                             denb_fl[:, 512 * (2 * c4 + 1):
                                     512 * (2 * c4 + 1) + 512],
                             start=True, stop=True).then_inc(s_mm.h, 1)
                P.on("tensor", gat_mm); s_mm.n += 1
                def gat_ap(e, s=s, c4=c4, wmm=s_mm.n, wod=s_odw.n):
                    e.wait_ge(s_mm.h, wmm)
                    if s > 0 and c4 == 0:
                        e.wait_ge(s_odw.h, wod)
                    e.reciprocal(out=pb[5][:, :], in_=pb[5][:, :])
                    e.drain()
                    e.tensor_mul(outd_sb[:, c4, :], numd_sb[:, c4, :],
                                 pb[5][:, :])
                    e.tensor_mul(outd_sb[:, c4, :], outd_sb[:, c4, :],
                                 g_sb[:, c4, :]).then_inc(s_gp.h, 1)
                P.on("vector", gat_ap); s_gp.n += 1
                def odwr(e, s=s, c4=c4, w=s_gp.n):
                    e.wait_ge(s_gp.h, w)
                    e.dma_start(out=ag_in[128 * c4:128 * (c4 + 1),
                                          512 * s:512 * (s + 1)],
                                in_=outd_sb[:, c4, :]).then_inc(s_odw.h, 16)
                P.on("gpsimd", odwr); s_odw.n += 16

        def agattn(e, w=s_odw.n):
            e.wait_ge(s_odw.h, w)
            e.collective_compute(
                "AllGather", ALU.bypass, replica_groups=RG,
                ins=[ag_in[:, :]], outs=[ag_out[:, :]]).then_inc(s_cc.h, 1)
        P.on("gpsimd", agattn); s_cc.n += 1

        for ss in range(8):
            for ch in range(8):
                def wol_(e, ss=ss, ch=ch, wcc=s_cc.n, wmm=s_mm.n):
                    e.wait_ge(s_cc.h, wcc)
                    if ss > 0 and ch == 0:
                        e.wait_ge(s_mm.h, wmm)
                    e.indirect_dma_start(
                        xnd_sb[:, ch, :], None,
                        ag_out.rearrange("a (b c) -> (a b) c", b=16)[:, :],
                        IndirectOffsetOnAxis(woidx_sb[:, ch:ch + 1], 0),
                        element_offset=512 * ss) \
                        .then_inc(s_wol.h, 16)
                P.on("gpsimd", wol_); s_wol.n += 16
            if ss == 0:
                d_wog = dbgout("wog", [128, 8, 512], BF16)
                if d_wog is not None:
                    def dmpwog(e, w=s_wol.n):
                        e.wait_ge(s_wol.h, w)
                        e.dma_start(out=d_wog, in_=xnd_sb[:, :, :]) \
                            .then_inc(s_fin.h, 16)
                    P.on("gpsimd", dmpwog); s_fin.n += 16
                    def wogwait(e, w=s_fin.n):
                        e.wait_ge(s_fin.h, w)
                    P.on("tensor", wogwait)
            for j in range(4):
                def womm(e, ss=ss, j=j, wl=s_wol.n, wsc=s_sc.n, wdl=s_dlw.n):
                    e.wait_ge(s_wol.h, wl)
                    if ss + j > 0:
                        e.wait_ge(s_sc.h, wsc)
                    for half in range(2):
                        for ch in range(8):
                            e.matmul(pb[half][:, :],
                                     xnd_sb[:, ch, 128 * j:128 * (j + 1)],
                                     wo_sb[:, ch, 512 * half:512 * (half + 1)],
                                     start=(ch == 0), stop=False)
                        m = e.matmul(pb[half][:, :], onesrow_sb[:, :],
                                     bias_sb[:, 2056 + 512 * half:
                                             2056 + 512 * (half + 1)],
                                     start=False, stop=True)
                    m.then_inc(s_mm.h, 1)
                P.on("tensor", womm); s_mm.n += 1
                def woev(e, ss=ss, j=j, wmm=s_mm.n, wdl=s_dlw.n):
                    e.wait_ge(s_mm.h, wmm)
                    if ss * 4 + j >= 2:
                        e.wait_ge(s_dlw.h, wdl)
                    e.copy(out=dtile_sb[:, j % 2, 0:512], in_=pb[0][:, :])
                    e.copy(out=dtile_sb[:, j % 2, 512:1024], in_=pb[1][:, :]) \
                        .then_inc(s_sc.h, 1)
                P.on("scalar", woev); s_sc.n += 1
                dl_tgt = s_dlw.n
                def dlwr(e, ss=ss, j=j, w=s_sc.n):
                    e.wait_ge(s_sc.h, w)
                    r0 = 512 * ss + 128 * j
                    e.dma_start(out=delta_hbm[r0:r0 + 128, :],
                                in_=dtile_sb[:, j % 2, :]).then_inc(s_dlw.h, 16)
                    if d_delta is not None:
                        e.dma_start(out=d_delta[r0:r0 + 128, :],
                                    in_=dtile_sb[:, j % 2, :]).then_inc(s_dlw.h, 16)
                P.on("gpsimd", dlwr)
                s_dlw.n += 16 * (2 if d_delta is not None else 1)

        d_agout = dbgout("agout", [NCORE * 512, T], BF16)
        if d_agout is not None:
            def dmpago(e, w=s_cc.n):
                e.wait_ge(s_cc.h, w)
                e.dma_start(out=d_agout, in_=ag_out).then_inc(s_fin.h, 16)
            P.on("gpsimd", dmpago); s_fin.n += 16
        d_xn = dbgout("xn", [T, D], BF16)
        d_qh = dbgout("qh", [T, 512], BF16)
        d_kh = dbgout("kh", [T, 512], BF16)
        d_agin = dbgout("agin", [512, T], BF16)
        for nm, dst, src in (("xn", d_xn, xn_hbm), ("qh", d_qh, q_hbm),
                             ("kh", d_kh, k_hbm), ("agin", d_agin, ag_in)):
            if dst is not None:
                def dmp(e, dst=dst, src=src, w=s_dlw.n):
                    e.wait_ge(s_dlw.h, w)
                    e.dma_start(out=dst, in_=src).then_inc(s_fin.h, 16)
                P.on("gpsimd", dmp); s_fin.n += 16

        def emit_p234():
            DW = float(8 * CAP)
            DLW_P1 = s_dlw.n
            for ch in range(8):
                def wrl(e, d=wrf_sb[:, ch, :], s=wr[128 * ch:128 * (ch + 1), :]):
                    e.dma_start(out=d, in_=s).then_inc(s_wld.h, 16)
                P.on("gpsimd", wrl); s_wld.n += 16
            def wbl(e):
                e.dma_start(out=b1e_sb[:, :], in_=b1e).then_inc(s_wld.h, 16)
                e.dma_start(out=b2e_sb[:, :], in_=b2e).then_inc(s_wld.h, 16)
                e.dma_start(out=fcw_sb[:, :, :], in_=fcw).then_inc(s_wld.h, 16)
                e.dma_start(out=fcb_sb[:, :], in_=fcb).then_inc(s_wld.h, 16)
                e.dma_start(out=eidx_sb[:, :], in_=eidx).then_inc(s_wld.h, 16)
                e.dma_start(out=ones8_sb[:, :], in_=c_ones8).then_inc(s_wld.h, 16)
                e.dma_start(out=ones8f_sb[:, :], in_=c_ones8f) \
                    .then_inc(s_wld.h, 16)
                e.dma_start(out=onescol_sb[:, :], in_=c_onescol).then_inc(s_wld.h, 16)
            P.on("gpsimd", wbl); s_wld.n += 128
            def wbl2(e):
                e.dma_start(out=onesrowf_sb[:, :], in_=c_onesrowf) \
                    .then_inc(s_wld.h, 16)
            P.on("sync", wbl2); s_wld.n += 16
            WLD_P2 = s_wld.n
            def fold2(e, w=WLD_P2, wfold=EV_FOLD):
                e.wait_ge(s_wld.h, w)
                e.wait_ge(s_ev.h, wfold)
                for ch in range(8):
                    e.tensor_scalar(out=wrf_sb[:, ch, :], in0=wrf_sb[:, ch, :],
                                    scalar1=n2_sb[:, ch:ch + 1], scalar2=None,
                                    op0=ALU.mult)
                e.memset(dumpc_sb[:, 0:1], DW)
                e.memset(dumpc_sb[:, 1:2], float(TPC))
                e.memset(dumpc_sb[:, 2:3], float(CAP))
                m = e.memset(dumpc_sb[:, 3:4], LN_EPS)
                m.then_inc(s_ev.h, 1)
            P.on("vector", fold2); s_ev.n += 1
            EV_FOLD2 = s_ev.n
            def zrows(e, w=EV_FOLD):
                e.wait_ge(s_ev.h, w)
                for half in range(2):
                    e.dma_start(out=xf_in[TPC:TPC + 1, 512 * half:512 * (half + 1)],
                                in_=zbig_sb[0:1, :]).then_inc(s_odw.h, 16)
                    e.dma_start(out=ob_in[CAP:CAP + 1, 512 * half:512 * (half + 1)],
                                in_=zbig_sb[0:1, :]).then_inc(s_odw.h, 16)
                zsrc = iscr_sb[:, :, :].rearrange("p c d -> p (c d)")
                flat = info_in.rearrange("a b -> (a b)")
                for k in range(4):
                    e.dma_start(
                        out=flat[20480 * k:20480 * (k + 1)]
                        .rearrange("(p c) -> p c", p=128),
                        in_=zsrc[:, 0:160]).then_inc(s_odw.h, 16)
                e.dma_start(
                    out=flat[81920:82048].rearrange("(p c) -> p c", p=128),
                    in_=zsrc[:, 0:1]).then_inc(s_odw.h, 16)
            P.on("gpsimd", zrows); s_odw.n += 144
            ODW_Z = s_odw.n

            dlw_per = DLW_P1 // 32
            for t in range(NTILE):
                odwn = s_odw.n
                def xl2(e, t=t, w=odwn, wdl=(t + 1) * dlw_per):
                    e.wait_ge(s_dlw.h, wdl)
                    if t >= 2:
                        e.wait_ge(s_odw.h, w - 16)
                    e.dma_start(out=x_tile[:, t % 2, :],
                                in_=x_my[128 * t:128 * (t + 1), :]) \
                        .then_inc(s_xld.h, 16)
                    e.dma_start(out=dtile_sb[:, t % 2, :],
                                in_=delta_hbm[128 * t:128 * (t + 1), :]) \
                        .then_inc(s_xld.h, 16)
                P.on("gpsimd", xl2); s_xld.n += 32
                def v2a(e, t=t, w=s_xld.n, wf=EV_FOLD2):
                    e.wait_ge(s_xld.h, w)
                    e.wait_ge(s_ev.h, wf)
                    e.tensor_add(scr32[:, :], x_tile[:, t % 2, :],
                                 dtile_sb[:, t % 2, :]).then_inc(s_ev.h, 1)
                P.on("vector", v2a); s_ev.n += 1
                o3 = (t % 2) * 3
                def s2a(e, t=t, w=s_ev.n, o3=o3):
                    e.wait_ge(s_ev.h, w)
                    e.activation(out=x_tile[:, t % 2, :], in_=scr32[:, :],
                                 func=AF.Square, accum_out=stat_sb[:, o3:o3 + 1])
                    e.activation(out=stat_sb[:, o3 + 1:o3 + 2],
                                 in_=stat_sb[:, o3:o3 + 1],
                                 func=AF.Sqrt, scale=1.0 / D).then_inc(s_sc.h, 1)
                P.on("scalar", s2a); s_sc.n += 1
                rtw = s_rt.n + 1
                def v2b(e, t=t, w=s_sc.n, o3=o3, rtw=rtw):
                    e.wait_ge(s_sc.h, w)
                    e.reciprocal(out=stat_sb[:, o3 + 2:o3 + 3],
                                 in_=stat_sb[:, o3 + 1:o3 + 2]).then_inc(s_rt.h, 1)
                    e.wait_ge(s_rt.h, rtw)
                    e.tensor_scalar(out=xf_tile[:, t % 2, :], in0=scr32[:, :],
                                    scalar1=stat_sb[:, o3 + 2:o3 + 3], scalar2=None,
                                    op0=ALU.mult).then_inc(s_ev.h, 1)
                P.on("vector", v2b); s_ev.n += 1; s_rt.n += 1
                def xfw(e, t=t, w=s_ev.n):
                    e.wait_ge(s_ev.h, w)
                    e.dma_start(out=xf_in[128 * t:128 * (t + 1), :],
                                in_=xf_tile[:, t % 2, :]).then_inc(s_odw.h, 16)
                P.on("gpsimd", xfw); s_odw.n += 16
            XFW_ALL = s_odw.n
            def agxf(e, w=XFW_ALL, wz=ODW_Z):
                e.wait_ge(s_odw.h, max(w, wz))
                e.collective_compute(
                    "AllGather", ALU.bypass, replica_groups=RG,
                    ins=[xf_in[:, :]], outs=[xf_all[:, :]]).then_inc(s_cc.h, 1)
            P.on("gpsimd", agxf); s_cc.n += 1
            CC_XF = s_cc.n
            d_xf = dbgout("xf", [XF_SH, D], BF16)
            if d_xf is not None:
                def dmpxf(e, w=CC_XF):
                    e.wait_ge(s_cc.h, w)
                    e.dma_start(out=d_xf, in_=xf_in).then_inc(s_fin.h, 16)
                P.on("gpsimd", dmpxf); s_fin.n += 16
            if phases <= 2:
                return

            d_rl = dbgout("rl", [128, 256], F32)
            d_prob = dbgout("prob", [128, 256], F32)
            d_rsc = dbgout("rsc", [128, 128], F32)
            dbg_fin = [s_fin.n]
            for s8 in range(8):
                for ch in range(8):
                    def trxf(e, s8=s8, ch=ch, w=XFW_ALL, wmm=s_mm.n):
                        e.wait_ge(s_odw.h, w)
                        if s8 >= 1 and ch == 0:
                            e.wait_ge(s_mm.h, wmm)
                        e.dma_start_transpose(
                            xnd_sb[:, ch, :],
                            xf_in[512 * s8:512 * (s8 + 1),
                                  128 * ch:128 * (ch + 1)]) \
                            .then_inc(s_trx.h, 16)
                    P.on("sync", trxf); s_trx.n += 16
                for j in range(4):
                    t = 4 * s8 + j
                    def rmm(e, t=t, j=j, wtr=s_trx.n, wl=WLD_P2, wev=s_ev.n,
                            wf=EV_FOLD2):
                        e.wait_ge(s_trx.h, wtr)
                        e.wait_ge(s_wld.h, wl)
                        if t == 0:
                            e.wait_ge(s_ev.h, wf)
                        if t > 0:
                            e.wait_ge(s_ev.h, wev)
                        for ch in range(8):
                            e.matmul(pb[0][:, 0:8],
                                     xnd_sb[:, ch, 128 * j:128 * (j + 1)],
                                     wrf_sb[:, ch, :], start=(ch == 0), stop=False)
                        e.matmul(pb[0][:, 0:8], onesrow_sb[:, :],
                                 bias_sb[:, 3080:3088], start=False, stop=True) \
                            .then_inc(s_mm.h, 1)
                    P.on("tensor", rmm); s_mm.n += 1
                    rt0 = s_rt.n
                    def rvec(e, t=t, w=s_mm.n, wfin=dbg_fin[0]):
                        e.wait_ge(s_mm.h, w)
                        if d_rl is not None:
                            if t > 0:
                                e.wait_ge(s_fin.h, wfin)
                            e.tensor_copy(out=prob2_sb[:, 24:32],
                                          in_=pb[0][:, 0:8])
                        e.reduce_max(rsc_sb[:, 0:1], pb[0][:, 0:8], axis=AX.X)
                        e.drain()
                        e.tensor_scalar(out=rsc_sb[:, 1:2], in0=rsc_sb[:, 0:1],
                                        scalar1=-1.0, scalar2=None, op0=ALU.mult) \
                            .then_inc(s_rt.h, 1)
                    P.on("vector", rvec); s_rt.n += 1
                    if d_rl is not None:
                        def drl(e, t=t, w=s_rt.n):
                            e.wait_ge(s_rt.h, w)
                            e.dma_start(out=d_rl[:, 8 * t:8 * (t + 1)],
                                        in_=prob2_sb[:, 24:32]) \
                                .then_inc(s_fin.h, 16)
                        P.on("gpsimd", drl); s_fin.n += 16
                        dbg_fin[0] = s_fin.n
                    if d_rsc is not None:
                        def drsc(e, t=t, w=s_rt.n):
                            e.wait_ge(s_rt.h, w)
                            e.dma_start(out=d_rsc[:, 4 * t:4 * (t + 1)],
                                        in_=rsc_sb[:, 0:4]).then_inc(s_fin.h, 16)
                        P.on("gpsimd", drsc); s_fin.n += 16
                        dbg_fin[0] = s_fin.n
                    def rsca(e, t=t, w=s_rt.n, wfin=dbg_fin[0]):
                        e.wait_ge(s_rt.h, w)
                        if d_prob is not None and t > 0:
                            e.wait_ge(s_fin.h, wfin)
                        e.activation(out=prob_sb[:, :], in_=pb[0][:, 0:8],
                                     func=AF.Exp, bias=rsc_sb[:, 1:2], scale=1.0) \
                            .then_inc(s_sc.h, 1)
                    P.on("scalar", rsca); s_sc.n += 1
                    if d_prob is not None:
                        def dprb(e, t=t, w=s_sc.n):
                            e.wait_ge(s_sc.h, w)
                            e.dma_start(out=d_prob[:, 8 * t:8 * (t + 1)],
                                        in_=prob_sb[:, :]).then_inc(s_fin.h, 16)
                        P.on("gpsimd", dprb); s_fin.n += 16
                        dbg_fin[0] = s_fin.n
                    rtA, rtB, rtC = s_rt.n + 1, s_rt.n + 2, s_rt.n + 3
                    def rvec2(e, t=t, w=s_sc.n, rtA=rtA, rtB=rtB, rtC=rtC,
                              wfin=dbg_fin[0]):
                        e.wait_ge(s_sc.h, w)
                        if d_rl is not None or d_prob is not None:
                            e.wait_ge(s_fin.h, wfin)
                        e.reduce_max(rsc_sb[:, 2:3], prob_sb[:, :], axis=AX.X) \
                            .then_inc(s_rt.h, 1)
                        e.wait_ge(s_rt.h, rtA)
                        e.tensor_scalar(out=rsc_sb[:, 4:12], in0=prob_sb[:, :],
                                        scalar1=rsc_sb[:, 2:3], scalar2=None,
                                        op0=ALU.is_equal)
                        e.drain()
                        e.tensor_mul(prob2_sb[:, 0:8], rsc_sb[:, 4:12],
                                     iota8_sb[:, :])
                        e.drain()
                        e.reduce_sum(route_sb[:, t, 0:1], prob2_sb[:, 0:8],
                                     axis=AX.X)
                        e.scalar_tensor_tensor(out=prob2_sb[:, 0:8],
                                               in0=rsc_sb[:, 4:12], scalar=-1e9,
                                               in1=prob_sb[:, :],
                                               op0=ALU.mult, op1=ALU.add)
                        e.drain()
                        e.reduce_max(rsc_sb[:, 3:4], prob2_sb[:, 0:8], axis=AX.X) \
                            .then_inc(s_rt.h, 1)
                        e.wait_ge(s_rt.h, rtB)
                        e.tensor_scalar(out=prob2_sb[:, 8:16], in0=prob2_sb[:, 0:8],
                                        scalar1=rsc_sb[:, 3:4], scalar2=None,
                                        op0=ALU.is_equal)
                        e.drain()
                        e.tensor_mul(prob2_sb[:, 16:24], prob2_sb[:, 8:16],
                                     iota8_sb[:, :])
                        e.drain()
                        e.reduce_sum(route_sb[:, t, 1:2], prob2_sb[:, 16:24],
                                     axis=AX.X)
                        e.tensor_add(rsc_sb[:, 0:1], rsc_sb[:, 2:3], rsc_sb[:, 3:4])
                        e.drain()
                        e.reciprocal(out=rsc_sb[:, 1:2], in_=rsc_sb[:, 0:1]) \
                            .then_inc(s_rt.h, 1)
                        e.wait_ge(s_rt.h, rtC)
                        e.tensor_scalar(out=route_sb[:, t, 2:3], in0=rsc_sb[:, 2:3],
                                        scalar1=rsc_sb[:, 1:2], scalar2=None,
                                        op0=ALU.mult)
                        e.tensor_scalar(out=route_sb[:, t, 3:4], in0=rsc_sb[:, 3:4],
                                        scalar1=rsc_sb[:, 1:2], scalar2=None,
                                        op0=ALU.mult)
                        e.tensor_add(prob2_sb[:, 24:32], rsc_sb[:, 4:12],
                                     prob2_sb[:, 8:16])
                        e.drain()
                        m = e.tensor_copy(out=ohb_sb[:, 0, :],
                                          in_=prob2_sb[:, 24:32])
                        m.then_inc(s_ev.h, 1)
                    P.on("vector", rvec2); s_rt.n += 3; s_ev.n += 1
                    def kmm(e, t=t, w=s_ev.n):
                        e.wait_ge(s_ev.h, w)
                        e.matmul(pb[6][:, 0:8], tri_sb[:, :], ohb_sb[:, 0, :],
                                 start=(t == 0), stop=False)
                        e.matmul(pb[7][0:8, t:t + 1], ohb_sb[:, 0, :],
                                 onescol_sb[:, :], start=True, stop=True) \
                            .then_inc(s_mm.h, 1)
                    P.on("tensor", kmm); s_mm.n += 1
                    def kev(e, t=t, w=s_mm.n):
                        e.wait_ge(s_mm.h, w)
                        e.tensor_mul(prob2_sb[:, 0:8], pb[6][:, 0:8],
                                     rsc_sb[:, 4:12])
                        e.drain()
                        e.reduce_sum(route_sb[:, t, 4:5], prob2_sb[:, 0:8],
                                     axis=AX.X)
                        e.tensor_mul(prob2_sb[:, 16:24], pb[6][:, 0:8],
                                     prob2_sb[:, 8:16])
                        e.drain()
                        m = e.reduce_sum(route_sb[:, t, 5:6], prob2_sb[:, 16:24],
                                         axis=AX.X)
                        m.then_inc(s_ev.h, 1)
                    P.on("vector", kev); s_ev.n += 1
                    def kmm2(e, t=t, w=s_ev.n):
                        e.wait_ge(s_ev.h, w)
                        e.matmul(pb[6][:, 0:8], tric_sb[:, :], ohb_sb[:, 0, :],
                                 start=False, stop=(t == NTILE - 1)) \
                            .then_inc(s_mm.h, 1)
                    P.on("tensor", kmm2); s_mm.n += 1
            EV_ROUTE = s_ev.n
            d_route = dbgout("route", [128, 256], F32)
            if d_route is not None:
                def dmprt(e, w=EV_ROUTE):
                    e.wait_ge(s_ev.h, w)
                    e.dma_start(out=d_route,
                                in_=route_sb[:, :, :].rearrange(
                                    "p c d -> p (c d)")).then_inc(s_fin.h, 16)
                P.on("gpsimd", dmprt); s_fin.n += 16
            if phases == 24:
                return
            def cev(e, w=s_mm.n):
                e.wait_ge(s_mm.h, w)
                e.tensor_copy(out=cnt_sb[:, 224:256], in_=pb[7][0:8, 0:32]) \
                    .then_inc(s_ev.h, 1)
            P.on("vector", cev); s_ev.n += 1
            def cwr(e, w=s_ev.n):
                e.wait_ge(s_ev.h, w)
                e.dma_start(out=cnt_in.rearrange("a (b c) -> (a b) c", b=8),
                            in_=cnt_sb[:, 224:256]).then_inc(s_odw.h, 16)
            P.on("gpsimd", cwr); s_odw.n += 16
            def agcnt(e, w=s_odw.n):
                e.wait_ge(s_odw.h, w)
                e.collective_compute(
                    "AllGather", ALU.bypass, replica_groups=RG,
                    ins=[cnt_in[:, :]], outs=[cnt_all[:, :]]).then_inc(s_cc.h, 1)
            P.on("gpsimd", agcnt); s_cc.n += 1
            def cld(e, w=s_cc.n):
                e.wait_ge(s_cc.h, w)
                e.dma_start(out=cnt_sb[:, :], in_=cnt_all[:, :]) \
                    .then_inc(s_xld.h, 16)
            P.on("gpsimd", cld); s_xld.n += 16
            def bvec(e, w=s_xld.n):
                e.wait_ge(s_xld.h, w)
                e.tensor_reduce(tot_sb[:, :],
                                cnt_sb[:, 0:256].rearrange("p (c d) -> p c d", c=8),
                                axis=AX.X, op=ALU.add)
                e.drain()
                m = e.tensor_scalar(out=tot_sb[:, :], in0=tot_sb[:, :],
                                    scalar1=cmask_sb[:, 0:1], scalar2=None,
                                    op0=ALU.mult)
                m.then_inc(s_ev.h, 1)
            P.on("vector", bvec); s_ev.n += 1
            def bmm(e, w=s_ev.n):
                e.wait_ge(s_ev.h, w)
                e.matmul(pb[5][:, 0:8], ones8f_sb[:, :], tot_sb[:, :],
                         start=True, stop=True).then_inc(s_mm.h, 1)
            P.on("tensor", bmm); s_mm.n += 1
            MM_BASE = s_mm.n
            if phases == 25:
                return

            for ch in range(8):
                def w1l(e, ch=ch, w=MM_BASE):
                    if ch == 0:
                        e.wait_ge(s_mm.h, w)
                    e.dma_start(out=w1e_sb[:, ch, :],
                                in_=w1e[128 * ch:128 * (ch + 1), :]) \
                        .then_inc(s_wld.h, 16)
                P.on("gpsimd", w1l); s_wld.n += 16
            for ch in range(32):
                def w2l(e, ch=ch):
                    e.dma_start(out=w2e_sb[:, ch, :],
                                in_=w2e[128 * ch:128 * (ch + 1), :]) \
                        .then_inc(s_wld.h, 16)
                P.on("gpsimd", w2l); s_wld.n += 16
            WLD_P3 = s_wld.n

            for t in range(NTILE):
                def p2b(e, t=t, wmm=MM_BASE, wodw=s_odw.n):
                    e.wait_ge(s_mm.h, wmm)
                    if t > 0:
                        e.wait_ge(s_odw.h, wodw)
                    for c in range(2):
                        e.tensor_scalar(out=rsc_sb[:, 4:12], in0=iota8_sb[:, :],
                                        scalar1=route_sb[:, t, c:c + 1],
                                        scalar2=None, op0=ALU.is_equal)
                        e.drain()
                        e.tensor_mul(prob2_sb[:, 0:8], pb[5][:, 0:8],
                                     rsc_sb[:, 4:12])
                        e.drain()
                        e.reduce_sum(rsc_sb[:, 0:1], prob2_sb[:, 0:8], axis=AX.X)
                        e.drain()
                        e.tensor_add(rsc_sb[:, 1:2], route_sb[:, t, 4 + c:5 + c],
                                     rsc_sb[:, 0:1])
                        e.drain()
                        e.tensor_scalar(out=rsc_sb[:, 2:3], in0=rsc_sb[:, 1:2],
                                        scalar1=float(CAP), scalar2=None,
                                        op0=ALU.is_lt)
                        e.scalar_tensor_tensor(out=rsc_sb[:, 3:4],
                                               in0=route_sb[:, t, c:c + 1],
                                               scalar=float(CAP),
                                               in1=rsc_sb[:, 1:2],
                                               op0=ALU.mult, op1=ALU.add)
                        e.drain()
                        e.tensor_copy(out=mki_sb[:, 0:1], in_=rsc_sb[:, 2:3])
                        e.drain()
                        e.select(rsc_sb[:, 3:4], mki_sb[:, 0:1], rsc_sb[:, 3:4],
                                 dumpc_sb[:, 0:1])
                        e.drain()
                        e.tensor_copy(out=sidx_sb[:, c:c + 1], in_=rsc_sb[:, 3:4])
                        e.tensor_scalar(out=vg_sb[:, c, 0:1], in0=iota_sb[:, :],
                                        scalar1=tokoff_sb[:, 0:1],
                                        scalar2=float(128 * t + 1),
                                        op0=ALU.add, op1=ALU.add)
                        e.tensor_mul(vg_sb[:, c, 1:2],
                                     route_sb[:, t, 2 + c:3 + c], rsc_sb[:, 2:3])
                        e.select(rsc_sb[:, 0:1], mki_sb[:, 0:1], rsc_sb[:, 1:2],
                                 dumpc_sb[:, 2:3])
                        e.drain()
                        e.scalar_tensor_tensor(out=rsc_sb[:, 0:1],
                                               in0=route_sb[:, t, c:c + 1],
                                               scalar=float(CAP + 1),
                                               in1=rsc_sb[:, 0:1],
                                               op0=ALU.mult, op1=ALU.add)
                        e.drain()
                        m = e.tensor_copy(out=gi_sb[:, t, c:c + 1],
                                          in_=rsc_sb[:, 0:1])
                    m.then_inc(s_ev.h, 1)
                P.on("vector", p2b); s_ev.n += 1
                def p2bs(e, t=t, w=s_ev.n, wz=ODW_Z):
                    e.wait_ge(s_ev.h, w)
                    if t == 0:
                        e.wait_ge(s_odw.h, wz)
                    for c in range(2):
                        e.indirect_dma_start(
                            info_in[:, :],
                            IndirectOffsetOnAxis(sidx_sb[:, c:c + 1], 0),
                            vg_sb[:, c, :], None,
                            compute_op=ALU.add).then_inc(s_odw.h, 16)
                P.on("gpsimd", p2bs); s_odw.n += 32
            def arinfo(e, w=s_odw.n):
                e.wait_ge(s_odw.h, w)
                e.collective_compute(
                    "AllReduce", ALU.add, replica_groups=RG,
                    ins=[info_in[:, :]], outs=[info_all[:, :]]).then_inc(s_cc.h, 1)
            P.on("gpsimd", arinfo); s_cc.n += 1
            CC_INFO = s_cc.n

            d_info = dbgout("info", [INFO_ROWS, 2], F32)
            if d_info is not None:
                def dmpinfo(e, w=CC_INFO):
                    e.wait_ge(s_cc.h, w)
                    e.dma_start(out=d_info, in_=info_all).then_inc(s_fin.h, 16)
                P.on("gpsimd", dmpinfo); s_fin.n += 16
            d_cnt = dbgout("cnt", [8, 256], F32)
            if d_cnt is not None:
                def dmpcnt(e, w=CC_INFO):
                    e.wait_ge(s_cc.h, w)
                    e.dma_start(out=d_cnt, in_=cnt_all).then_inc(s_fin.h, 16)
                P.on("gpsimd", dmpcnt); s_fin.n += 16

            def fold_w1(e, w=WLD_P3):
                e.wait_ge(s_wld.h, w)
                for ch in range(8):
                    m = e.tensor_scalar(out=w1e_sb[:, ch, :], in0=w1e_sb[:, ch, :],
                                        scalar1=n2_sb[:, ch:ch + 1], scalar2=None,
                                        op0=ALU.mult)
                m.then_inc(s_ev.h, 1)
            P.on("vector", fold_w1); s_ev.n += 1
            if phases <= 3:
                return

            EV_P2_DONE = s_ev.n
            first_tld = [True]
            for wi, (wsrc, nrt, nct) in enumerate(((fw1, 32, 2), (fw2, 8, 8))):
                for tt in range(nrt):
                    for cc in range(nct):
                        def twl(e, wsrc=wsrc, tt=tt, cc=cc, w=EV_P2_DONE,
                                first=first_tld[0], wev=s_ev.n):
                            if first:
                                e.wait_ge(s_ev.h, w)
                            else:
                                e.wait_ge(s_ev.h, wev)
                            e.dma_start(out=scr32[:, 0:512],
                                        in_=wsrc[128 * tt:128 * (tt + 1),
                                                 512 * cc:512 * (cc + 1)]) \
                                .then_inc(s_tld.h, 16)
                        P.on("sync", twl); s_tld.n += 16
                        first_tld[0] = False
                        def tva(e, wi=wi, tt=tt, cc=cc, w=s_tld.n):
                            e.wait_ge(s_tld.h, w)
                            e.tensor_reduce(wacc_sb[:, 2:3], scr32[:, 0:512],
                                            axis=AX.X, op=ALU.add,
                                            apply_absolute_value=True)
                            e.drain()
                            if wi + tt + cc == 0:
                                m = e.tensor_copy(out=wacc_sb[:, 0:1],
                                                  in_=wacc_sb[:, 2:3])
                            elif wi == 1 and tt == 0 and cc == 0:
                                m = e.tensor_copy(out=wacc_sb[:, 1:2],
                                                  in_=wacc_sb[:, 2:3])
                            else:
                                m = e.tensor_add(wacc_sb[:, wi:wi + 1],
                                                 wacc_sb[:, wi:wi + 1],
                                                 wacc_sb[:, 2:3])
                            m.then_inc(s_ev.h, 1)
                        P.on("vector", tva); s_ev.n += 1
            EV_TVA = s_ev.n
            def tsmm(e, w=s_ev.n):
                e.wait_ge(s_ev.h, w)
                e.matmul(pb[4][0:1, 0:2], onescolf_sb[:, :], wacc_sb[:, 0:2],
                         start=True, stop=True).then_inc(s_mm.h, 1)
            P.on("tensor", tsmm); s_mm.n += 1
            def tsev(e, w=s_mm.n):
                e.wait_ge(s_mm.h, w)
                e.tensor_copy(out=wsum_sb[:, 0:2], in_=pb[4][0:1, 0:2]) \
                    .then_inc(s_ev.h, 1)
            P.on("vector", tsev); s_ev.n += 1
            def tscmm(e, w=s_ev.n):
                e.wait_ge(s_ev.h, w)
                e.matmul(pb[4][:, 0:2], onesrowf_sb[:, :], wsum_sb[0:1, 0:2],
                         start=True, stop=True).then_inc(s_mm.h, 1)
            P.on("tensor", tscmm); s_mm.n += 1
            rtT = s_rt.n + 1
            def tscv(e, w=s_mm.n, rtT=rtT):
                e.wait_ge(s_mm.h, w)
                e.tensor_scalar(out=wsc_sb[:, 2:3], in0=pb[4][:, 0:1],
                                scalar1=1.0 / (HID * D), scalar2=1e-5,
                                op0=ALU.mult, op1=ALU.max)
                e.tensor_scalar(out=wsc_sb[:, 3:4], in0=pb[4][:, 1:2],
                                scalar1=1.0 / (HID * D), scalar2=1e-5,
                                op0=ALU.mult, op1=ALU.max).then_inc(s_rt.h, 1)
                e.wait_ge(s_rt.h, rtT)
                e.reciprocal(out=wsc_sb[:, 0:1], in_=wsc_sb[:, 2:3])
                m = e.reciprocal(out=wsc_sb[:, 1:2], in_=wsc_sb[:, 3:4])
                m.then_inc(s_rt.h, 1)
            P.on("vector", tscv); s_rt.n += 2
            TRT = s_rt.n
            def fconv(e, w=TRT):
                e.wait_ge(s_rt.h, w)
                m = e.tensor_scalar(
                    out=fcw_sb[:, :, :].rearrange("p c d -> p (c d)"),
                    in0=fcw_sb[:, :, :].rearrange("p c d -> p (c d)"),
                    scalar1=wsc_sb[:, 2:3], scalar2=None, op0=ALU.mult)
                m.then_inc(s_ev.h, 1)
            P.on("vector", fconv); s_ev.n += 1
            first2 = [True]
            for wi, (wsrc, wdst, nrt, nct) in enumerate(
                    ((fw1, w1t_hbm, 32, 2), (fw2, w2t_hbm, 8, 8))):
                for tt in range(nrt):
                    for cc in range(nct):
                        def twl2(e, wsrc=wsrc, tt=tt, cc=cc, w=s_tld.n + 32,
                                 first=first2[0], wtva=EV_TVA):
                            if first:
                                e.wait_ge(s_ev.h, wtva)
                            e.wait_ge(s_tld.h, w - 32)
                            e.dma_start(out=scr32[:, 0:512],
                                        in_=wsrc[128 * tt:128 * (tt + 1),
                                                 512 * cc:512 * (cc + 1)]) \
                                .then_inc(s_tld.h, 16)
                        P.on("sync", twl2); s_tld.n += 16
                        first2[0] = False
                        def tvb(e, wi=wi, w=s_tld.n, wrt=TRT):
                            e.wait_ge(s_tld.h, w)
                            e.wait_ge(s_rt.h, wrt)
                            e.tensor_scalar(out=scr32[:, 0:512],
                                            in0=scr32[:, 0:512],
                                            scalar1=wsc_sb[:, wi:wi + 1],
                                            scalar2=None, op0=ALU.mult)
                            e.drain()
                            e.tensor_copy(out=hq_i32[:, :], in_=scr32[:, 0:512])
                            e.drain()
                            e.tensor_scalar(out=hq_i32[:, :], in0=hq_i32[:, :],
                                            scalar1=1, scalar2=-1,
                                            op0=ALU.min, op1=ALU.max)
                            e.drain()
                            m = e.tensor_copy(out=ternb_sb[:, :], in_=hq_i32[:, :])
                            m.then_inc(s_ev.h, 1)
                        P.on("vector", tvb); s_ev.n += 1
                        def twr2(e, wdst=wdst, tt=tt, cc=cc, w=s_ev.n):
                            e.wait_ge(s_ev.h, w)
                            e.dma_start(out=wdst[128 * tt:128 * (tt + 1),
                                                 512 * cc:512 * (cc + 1)],
                                        in_=ternb_sb[:, :]).then_inc(s_tld.h, 16)
                        P.on("gpsimd", twr2); s_tld.n += 16
            TERN_TLD = s_tld.n
            d_w1t = dbgout("w1t", [HID, D], BF16)
            d_w2t = dbgout("w2t", [D, HID], BF16)
            for dst, src_ in ((d_w1t, w1t_hbm), (d_w2t, w2t_hbm)):
                if dst is not None:
                    def dmpt(e, dst=dst, src_=src_, wv=TERN_TLD):
                        e.wait_ge(s_tld.h, wv)
                        e.dma_start(out=dst, in_=src_).then_inc(s_fin.h, 16)
                    P.on("gpsimd", dmpt); s_fin.n += 16
            if phases == 35:
                return

            for j in range(40):
                def ird(e, j=j, w=CC_INFO, wl=WLD_P2):
                    if j == 0:
                        e.wait_ge(s_cc.h, w)
                        e.wait_ge(s_wld.h, wl)
                    e.indirect_dma_start(
                        info_sb[:, j, :], None, info_all[:, :],
                        IndirectOffsetOnAxis(eidx_sb[:, 0:1], 0),
                        element_offset=256 * j).then_inc(s_xld.h, 16)
                P.on("gpsimd", ird); s_xld.n += 16
            def iprep(e, w=s_xld.n, wcc=CC_XF, wf=EV_FOLD2):
                e.wait_ge(s_xld.h, w)
                e.wait_ge(s_cc.h, wcc)
                e.wait_ge(s_ev.h, wf)
                tbl = info_sb[:, :, 0:1].rearrange("p c d -> p (c d)")
                e.tensor_scalar(out=iscr_sb[:, :, 0], in0=tbl, scalar1=0.5,
                                scalar2=None, op0=ALU.is_lt)
                e.tensor_scalar(out=iscr_sb[:, :, 1], in0=tbl, scalar1=-1.0,
                                scalar2=None, op0=ALU.add)
                e.drain()
                e.tensor_scalar(out=iscr_sb[:, :, 2], in0=iscr_sb[:, :, 1],
                                scalar1=1.0 / TPC, scalar2=-0.5 + 1.0 / 8192,
                                op0=ALU.mult, op1=ALU.add)
                e.drain()
                e.tensor_copy(out=gidx_sb[:, :], in_=iscr_sb[:, :, 2])
                e.drain()
                e.tensor_copy(out=iscr_sb[:, :, 2], in_=gidx_sb[:, :])
                e.drain()
                e.tensor_add(iscr_sb[:, :, 3], iscr_sb[:, :, 1], iscr_sb[:, :, 2])
                e.tensor_copy(out=gidx_sb[:, :], in_=iscr_sb[:, :, 0])
                e.drain()
                e.select(iscr_sb[:, :, 3], gidx_sb[:, :],
                         dumpc_sb[:, 1:2].to_broadcast([128, 40]),
                         iscr_sb[:, :, 3])
                e.drain()
                m = e.tensor_copy(out=gidx_sb[:, :], in_=iscr_sb[:, :, 3])
                m.then_inc(s_ev.h, 1)
            P.on("vector", iprep); s_ev.n += 1
            IPREP_EV = s_ev.n

            NCHK = 10
            for kk in range(NCHK):
                for g in range(4):
                    def g3(e, kk=kk, g=g, w=IPREP_EV, wod=s_odw.n):
                        if g == 0:
                            e.wait_ge(s_ev.h, w)
                            if kk > 0:
                                e.wait_ge(s_odw.h, wod)
                        e.indirect_dma_start(
                            gbuf_sb[:, g, :], None, xf_all[:, :],
                            IndirectOffsetOnAxis(
                                gidx_sb[:, 4 * kk + g:4 * kk + g + 1], 0)) \
                            .then_inc(s_wol.h, 16)
                    P.on("gpsimd", g3); s_wol.n += 16
                def stg(e, kk=kk, w=s_wol.n):
                    e.wait_ge(s_wol.h, w)
                    for g in range(4):
                        e.dma_start(out=stage_hbm[512 * kk + 128 * g:
                                                  512 * kk + 128 * (g + 1), :],
                                    in_=gbuf_sb[:, g, :]).then_inc(s_odw.h, 16)
                P.on("gpsimd", stg); s_odw.n += 64
                for ch in range(8):
                    def t3(e, kk=kk, ch=ch, w=s_odw.n, wmm=s_mm.n):
                        e.wait_ge(s_odw.h, w)
                        if kk > 0 and ch == 0:
                            e.wait_ge(s_mm.h, wmm)
                        e.dma_start_transpose(
                            xq3_sb[:, ch, :],
                            stage_hbm[512 * kk:512 * (kk + 1),
                                      128 * ch:128 * (ch + 1)]) \
                            .then_inc(s_trx.h, 16)
                    P.on("sync", t3); s_trx.n += 16
                for hh in range(32):
                    def m31(e, kk=kk, hh=hh, w=s_trx.n, wl=WLD_P3, wsc=s_sc.n):
                        e.wait_ge(s_trx.h, w)
                        if kk + hh == 0:
                            e.wait_ge(s_wld.h, wl)
                        if kk + hh > 0:
                            e.wait_ge(s_sc.h, wsc)
                        bank = hh % 2
                        for ch in range(8):
                            m = e.matmul(pb[bank][:, :],
                                         w1e_sb[:, ch, 128 * hh:128 * (hh + 1)],
                                         xq3_sb[:, ch, :],
                                         start=(ch == 0), stop=(ch == 7))
                        m.then_inc(s_mm.h, 1)
                    P.on("tensor", m31); s_mm.n += 1
                    def e31(e, kk=kk, hh=hh, w=s_mm.n):
                        e.wait_ge(s_mm.h, w)
                        e.activation(out=hgel_sb[:, hh, :], in_=pb[hh % 2][:, :],
                                     func=AF.Gelu, bias=b1e_sb[:, hh:hh + 1],
                                     scale=1.0).then_inc(s_sc.h, 1)
                    P.on("scalar", e31); s_sc.n += 1
                for g in range(4):
                    def m32(e, kk=kk, g=g, wsc=s_sc.n, wev=s_ev.n):
                        e.wait_ge(s_sc.h, wsc)
                        if kk + g > 0:
                            e.wait_ge(s_ev.h, wev)
                        for dh in range(2):
                            for ch in range(32):
                                e.matmul(pb[2 + dh][:, :],
                                         hgel_sb[:, ch, 128 * g:128 * (g + 1)],
                                         w2e_sb[:, ch, 512 * dh:512 * (dh + 1)],
                                         start=(ch == 0), stop=False)
                            m = e.matmul(pb[2 + dh][:, :], onesrow_sb[:, :],
                                         b2e_sb[:, 512 * dh:512 * (dh + 1)],
                                         start=False, stop=True)
                        m.then_inc(s_mm.h, 1)
                    P.on("tensor", m32); s_mm.n += 1
                    def e32(e, kk=kk, g=g, w=s_mm.n, wod=s_odw.n):
                        e.wait_ge(s_mm.h, w)
                        if kk + g > 0:
                            e.wait_ge(s_odw.h, wod)
                        gslot = 4 * kk + g
                        e.tensor_scalar(out=orow_sb[:, 0:512], in0=pb[2][:, :],
                                        scalar1=info_sb[:, gslot, 1:2],
                                        scalar2=None, op0=ALU.mult)
                        m = e.tensor_scalar(out=orow_sb[:, 512:1024],
                                            in0=pb[3][:, :],
                                            scalar1=info_sb[:, gslot, 1:2],
                                            scalar2=None, op0=ALU.mult)
                        m.then_inc(s_ev.h, 1)
                    P.on("vector", e32); s_ev.n += 1
                    def w32(e, kk=kk, g=g, w=s_ev.n):
                        e.wait_ge(s_ev.h, w)
                        e.dma_start(out=ob_in[512 * kk + 128 * g:
                                              512 * kk + 128 * (g + 1), :],
                                    in_=orow_sb[:, :]).then_inc(s_odw.h, 16)
                    P.on("gpsimd", w32); s_odw.n += 16
            def agob(e, w=s_odw.n):
                e.wait_ge(s_odw.h, w)
                e.collective_compute(
                    "AllGather", ALU.bypass, replica_groups=RG,
                    ins=[ob_in[:, :]], outs=[ob_all[:, :]]).then_inc(s_cc.h, 1)
            P.on("gpsimd", agob); s_cc.n += 1
            CC_OB = s_cc.n
            d_ob = dbgout("ob", [OUT_ROWS, D], BF16)
            if d_ob is not None:
                def dmpob(e, w=CC_OB):
                    e.wait_ge(s_cc.h, w)
                    e.dma_start(out=d_ob, in_=ob_in).then_inc(s_fin.h, 16)
                P.on("gpsimd", dmpob); s_fin.n += 16
            d_stg = dbgout("stg", [CAP, D], BF16)
            if d_stg is not None:
                def dmpstg(e, w=CC_OB):
                    e.wait_ge(s_cc.h, w)
                    e.dma_start(out=d_stg, in_=stage_hbm).then_inc(s_fin.h, 16)
                P.on("gpsimd", dmpstg); s_fin.n += 16
            if phases <= 4:
                return

            def zq(e, w=EV_FOLD):
                e.wait_ge(s_ev.h, w)
                for half in range(2):
                    e.dma_start(out=xq_hbm[0:16, 512 * half:512 * (half + 1)],
                                in_=zbig_sb[0:16, :]).then_inc(s_tld.h, 16)
                    e.dma_start(out=xq_hbm[TPC + 16:TPC + 32,
                                           512 * half:512 * (half + 1)],
                                in_=zbig_sb[0:16, :]).then_inc(s_tld.h, 16)
            P.on("gpsimd", zq); s_tld.n += 64
            tile_order = [0, NTILE - 1] + list(range(1, NTILE - 1))
            halo_done_tld = [0]
            for it, t in enumerate(tile_order):
                def l4(e, t=t, it=it, w=CC_OB, wev=s_ev.n):
                    if it == 0:
                        e.wait_ge(s_cc.h, w)
                    if it >= 1:
                        e.wait_ge(s_ev.h, wev)
                    e.dma_start(out=x_tile[:, 0, :],
                                in_=x_my[128 * t:128 * (t + 1), :]) \
                        .then_inc(s_xld.h, 16)
                    e.dma_start(out=p4a_sb[:, 0, :],
                                in_=delta_hbm[128 * t:128 * (t + 1), :]) \
                        .then_inc(s_xld.h, 16)
                    e.indirect_dma_start(
                        p4a_sb[:, 1, :], None, ob_all[:, :],
                        IndirectOffsetOnAxis(gi_sb[:, t, 0:1], 0)) \
                        .then_inc(s_xld.h, 16)
                    e.indirect_dma_start(
                        p4a_sb[:, 2, :], None, ob_all[:, :],
                        IndirectOffsetOnAxis(gi_sb[:, t, 1:2], 0)) \
                        .then_inc(s_xld.h, 16)
                P.on("gpsimd", l4); s_xld.n += 64
                def v4a(e, t=t, w=s_xld.n):
                    e.wait_ge(s_xld.h, w)
                    e.scalar_tensor_tensor(out=y32[:, :], in0=x_tile[:, 0, :],
                                           scalar=2.0, in1=p4a_sb[:, 0, :],
                                           op0=ALU.mult, op1=ALU.add)
                    e.tensor_add(scr32[:, :], p4a_sb[:, 1, :], p4a_sb[:, 2, :])
                    e.drain()
                    e.tensor_add(y32[:, :], y32[:, :], scr32[:, :])
                    e.drain()
                    m = e.reduce_sum(stat_sb[:, 8:9], y32[:, :], axis=AX.X)
                    m.then_inc(s_ev.h, 1)
                P.on("vector", v4a); s_ev.n += 1
                def s4a(e, w=s_ev.n):
                    e.wait_ge(s_ev.h, w)
                    e.activation(out=scr32[:, :], in_=y32[:, :], func=AF.Square,
                                 accum_out=stat_sb[:, 9:10]).then_inc(s_sc.h, 1)
                P.on("scalar", s4a); s_sc.n += 1
                rt1 = s_rt.n + 1
                def v4b(e, w=s_sc.n, rt1=rt1):
                    e.wait_ge(s_sc.h, w)
                    e.tensor_scalar(out=stat_sb[:, 10:11], in0=stat_sb[:, 8:9],
                                    scalar1=1.0 / D, scalar2=None, op0=ALU.mult)
                    e.drain()
                    e.tensor_mul(stat_sb[:, 11:12], stat_sb[:, 10:11],
                                 stat_sb[:, 10:11])
                    e.drain()
                    e.scalar_tensor_tensor(out=stat_sb[:, 12:13],
                                           in0=stat_sb[:, 9:10], scalar=1.0 / D,
                                           in1=stat_sb[:, 11:12],
                                           op0=ALU.mult, op1=ALU.subtract) \
                        .then_inc(s_rt.h, 1)
                    e.wait_ge(s_rt.h, rt1)
                P.on("vector", v4b); s_rt.n += 1
                def s4b(e, w=s_rt.n):
                    e.wait_ge(s_rt.h, w)
                    e.activation(out=stat_sb[:, 13:14], in_=stat_sb[:, 12:13],
                                 func=AF.Sqrt, bias=dumpc_sb[:, 3:4], scale=1.0) \
                        .then_inc(s_sc.h, 1)
                P.on("scalar", s4b); s_sc.n += 1
                rtA, rtB, rtC = s_rt.n + 1, s_rt.n + 2, s_rt.n + 3
                def v4c(e, t=t, w=s_sc.n, rtA=rtA, rtB=rtB, rtC=rtC):
                    e.wait_ge(s_sc.h, w)
                    e.reciprocal(out=stat_sb[:, 14:15], in_=stat_sb[:, 13:14]) \
                        .then_inc(s_rt.h, 1)
                    e.wait_ge(s_rt.h, rtA)
                    e.tensor_scalar(out=p4a_sb[:, 3, :], in0=y32[:, :],
                                    scalar1=stat_sb[:, 10:11],
                                    scalar2=stat_sb[:, 14:15],
                                    op0=ALU.subtract, op1=ALU.mult)
                    e.drain()
                    e.tensor_reduce(stat_sb[:, 8:9], p4a_sb[:, 3, :], axis=AX.X,
                                    op=ALU.max, apply_absolute_value=True)
                    e.drain()
                    e.tensor_scalar(out=stat_sb[:, 8:9], in0=stat_sb[:, 8:9],
                                    scalar1=1e-5, scalar2=None, op0=ALU.max) \
                        .then_inc(s_rt.h, 1)
                    e.wait_ge(s_rt.h, rtB)
                    e.reciprocal(out=stat_sb[:, 9:10], in_=stat_sb[:, 8:9])
                    e.tensor_scalar(out=rcp1_sb[:, t:t + 1], in0=stat_sb[:, 8:9],
                                    scalar1=1.0 / 127.0, scalar2=None,
                                    op0=ALU.mult).then_inc(s_rt.h, 1)
                    e.wait_ge(s_rt.h, rtC)
                    for half in range(2):
                        e.tensor_scalar(
                            out=hq_i32[:, :],
                            in0=p4a_sb[:, 3, 512 * half:512 * (half + 1)],
                            scalar1=stat_sb[:, 9:10], scalar2=127.0,
                            op0=ALU.mult, op1=ALU.mult)
                        e.drain()
                        e.tensor_copy(
                            out=p4a_sb[:, 4, 512 * half:512 * (half + 1)],
                            in_=hq_i32[:, :])
                    e.drain()
                    m = e.tensor_scalar(out=p4a_sb[:, 4, :], in0=p4a_sb[:, 4, :],
                                        scalar1=rcp1_sb[:, t:t + 1], scalar2=None,
                                        op0=ALU.mult)
                    m.then_inc(s_ev.h, 1)
                P.on("vector", v4c); s_rt.n += 3; s_ev.n += 1
                def w4a(e, t=t, w=s_ev.n):
                    e.wait_ge(s_ev.h, w)
                    e.dma_start(out=x1_hbm[128 * t:128 * (t + 1), :],
                                in_=p4a_sb[:, 3, :]).then_inc(s_tld.h, 16)
                    e.dma_start(out=xq_hbm[16 + 128 * t:16 + 128 * (t + 1), :],
                                in_=p4a_sb[:, 4, :]).then_inc(s_tld.h, 16)
                    if t == 0:
                        e.dma_start(out=halo_in[0:1, :],
                                    in_=p4a_sb[0:1, 4, :]).then_inc(s_tld.h, 16)
                    if t == NTILE - 1:
                        e.dma_start(out=halo_in[1:2, :],
                                    in_=p4a_sb[127:128, 4, :]).then_inc(s_tld.h, 16)
                P.on("gpsimd", w4a)
                s_tld.n += 32 + (16 if t in (0, NTILE - 1) else 0)
                if it == 1:
                    def aghalo(e, w=s_tld.n):
                        e.wait_ge(s_tld.h, w)
                        e.collective_compute(
                            "AllGather", ALU.bypass, replica_groups=RG,
                            ins=[halo_in[:, :]], outs=[halo_all[:, :]]) \
                            .then_inc(s_cc.h, 1)
                    P.on("gpsimd", aghalo); s_cc.n += 1
            CC_HALO = s_cc.n
            def hl1(e, w=CC_HALO):
                e.wait_ge(s_cc.h, w)
                e.dma_start(out=halo16_sb[:, :], in_=halo_all[:, :]) \
                    .then_inc(s_xld.h, 16)
            P.on("gpsimd", hl1); s_xld.n += 16
            def hlmm(e, w=s_xld.n):
                e.wait_ge(s_xld.h, w)
                e.matmul(pb[4][0:2, :], halosel_sb[:, :],
                         halo16_sb[:, 0:512], start=True, stop=True)
                e.matmul(pb[5][0:2, :], halosel_sb[:, :],
                         halo16_sb[:, 512:1024], start=True, stop=True) \
                    .then_inc(s_mm.h, 1)
            P.on("tensor", hlmm); s_mm.n += 1
            def hlev(e, w=s_mm.n):
                e.wait_ge(s_mm.h, w)
                e.tensor_copy(out=hrow_sb[:, 0:512], in_=pb[4][0:2, :])
                m = e.tensor_copy(out=hrow_sb[:, 512:1024], in_=pb[5][0:2, :])
                m.then_inc(s_ev.h, 1)
            P.on("vector", hlev); s_ev.n += 1
            def hlw(e, w=s_ev.n):
                e.wait_ge(s_ev.h, w)
                e.dma_start(out=xq_hbm[15:16, :], in_=hrow_sb[0:1, :]) \
                    .then_inc(s_tld.h, 16)
                e.dma_start(out=xq_hbm[TPC + 16:TPC + 17, :],
                            in_=hrow_sb[1:2, :]).then_inc(s_tld.h, 16)
            P.on("gpsimd", hlw); s_tld.n += 32
            P4A_TLD = s_tld.n
            d_x1 = dbgout("x1", [TPC, D], BF16)
            d_xq = dbgout("xq", [TPC + 32, D], BF16)
            for dst, src_ in ((d_x1, x1_hbm), (d_xq, xq_hbm)):
                if dst is not None:
                    def dmp4(e, dst=dst, src_=src_, wv=P4A_TLD):
                        e.wait_ge(s_tld.h, wv)
                        e.dma_start(out=dst, in_=src_).then_inc(s_fin.h, 16)
                    P.on("gpsimd", dmp4); s_fin.n += 16
            if phases <= 5:
                return

            for ch in range(8):
                def f1l(e, ch=ch, w=TERN_TLD, wmm=s_mm.n):
                    if ch == 0:
                        e.wait_ge(s_tld.h, w)
                        e.wait_ge(s_mm.h, wmm)
                    e.dma_start_transpose(
                        fw1t_sb[:, ch, :],
                        w1t_hbm[:, 128 * ch:128 * (ch + 1)]) \
                        .then_inc(s_trx.h, 16)
                P.on("sync", f1l); s_trx.n += 16
            for ch in range(32):
                def f2l(e, ch=ch):
                    e.dma_start_transpose(
                        fw2t_sb[:, ch, :],
                        w2t_hbm[:, 128 * ch:128 * (ch + 1)]) \
                        .then_inc(s_trx.h, 16)
                P.on("sync", f2l); s_trx.n += 16
            FWL_TRX = s_trx.n
            nslab4 = phases - 50 if 50 < phases < 60 else 8
            for s in range(nslab4):
                for ch in range(8):
                    def t4(e, s=s, ch=ch, w=P4A_TLD, wtr=FWL_TRX, wmm=s_mm.n):
                        if ch == 0:
                            e.wait_ge(s_tld.h, w)
                            e.wait_ge(s_trx.h, wtr)
                            if s > 0:
                                e.wait_ge(s_mm.h, wmm)
                        e.dma_start_transpose(
                            xqf_sb[:, ch, 0:528],
                            xq_hbm[512 * s + 14:512 * s + 542,
                                   128 * ch:128 * (ch + 1)]) \
                            .then_inc(s_trd.h, 16)
                    P.on("sync", t4); s_trd.n += 16
                for hh in range(32):
                    def m41(e, s=s, hh=hh, w=s_trd.n, wsc=s_sc.n, wev=s_ev.n):
                        e.wait_ge(s_trd.h, w)
                        if s + hh > 0:
                            e.wait_ge(s_sc.h, wsc)
                            e.wait_ge(s_ev.h, wev)
                        for half in range(2):
                            for ch in range(8):
                                m = e.matmul(
                                    pb[half][:, 0:258],
                                    fw1t_sb[:, ch, 128 * hh:128 * (hh + 1)],
                                    xqf_sb[:, ch, 258 * half:258 * (half + 1)],
                                    start=(ch == 0), stop=(ch == 7))
                        m.then_inc(s_mm.h, 1)
                    P.on("tensor", m41); s_mm.n += 1
                    def e41(e, s=s, hh=hh, w=s_mm.n):
                        e.wait_ge(s_mm.h, w)
                        e.copy(out=h_sb[:, hh, 0:258], in_=pb[0][:, 0:258])
                        e.copy(out=h_sb[:, hh, 258:516], in_=pb[1][:, 0:258]) \
                            .then_inc(s_sc.h, 1)
                    P.on("scalar", e41); s_sc.n += 1
                    def c41(e, s=s, hh=hh, w=s_sc.n):
                        e.wait_ge(s_sc.h, w)
                        e.tensor_scalar(out=cvt_sb[:, :], in0=h_sb[:, hh, 1:513],
                                        scalar1=fcw_sb[:, hh, 0:1], scalar2=None,
                                        op0=ALU.mult)
                        e.drain()
                        e.scalar_tensor_tensor(out=cvt_sb[:, :],
                                               in0=h_sb[:, hh, 2:514],
                                               scalar=fcw_sb[:, hh, 1:2],
                                               in1=cvt_sb[:, :],
                                               op0=ALU.mult, op1=ALU.add)
                        e.drain()
                        m = e.scalar_tensor_tensor(out=cvt_sb[:, :],
                                                   in0=h_sb[:, hh, 3:515],
                                                   scalar=fcw_sb[:, hh, 2:3],
                                                   in1=cvt_sb[:, :],
                                                   op0=ALU.mult, op1=ALU.add)
                        m.then_inc(s_ev.h, 1)
                    P.on("vector", c41); s_ev.n += 1
                    def si41(e, s=s, hh=hh, w=s_ev.n):
                        e.wait_ge(s_ev.h, w)
                        e.activation(out=h_sb[:, hh, 2:514], in_=cvt_sb[:, :],
                                     func=AF.Silu, bias=fcb_sb[:, hh:hh + 1],
                                     scale=1.0).then_inc(s_sc.h, 1)
                    P.on("scalar", si41); s_sc.n += 1
                def q4a(e, s=s, w=s_sc.n):
                    e.wait_ge(s_sc.h, w)
                    m = e.tensor_reduce(
                        m32_sb[:, :],
                        h_sb[:, :, 2:514].rearrange("p c d -> p d c"),
                        axis=AX.X, op=ALU.max, apply_absolute_value=True)
                    m.then_inc(s_ev.h, 1)
                P.on("vector", q4a); s_ev.n += 1
                for g in range(4):
                    def q4t(e, g=g, w=s_ev.n):
                        e.wait_ge(s_ev.h, w)
                        e.transpose(pb[5][:, 0:128],
                                    m32_sb[:, 128 * g:128 * (g + 1)],
                                    ident_sb[:, :]).then_inc(s_mm.h, 1)
                    P.on("tensor", q4t); s_mm.n += 1
                    rtE = s_rt.n + 1
                    def q4r(e, g=g, w=s_mm.n, rtE=rtE):
                        e.wait_ge(s_mm.h, w)
                        e.tensor_reduce(stat_sb[:, 8:9], pb[5][:, 0:128],
                                        axis=AX.X, op=ALU.max)
                        e.drain()
                        e.tensor_scalar(out=stat_sb[:, 8:9], in0=stat_sb[:, 8:9],
                                        scalar1=1e-5, scalar2=None, op0=ALU.max) \
                            .then_inc(s_rt.h, 1)
                        e.wait_ge(s_rt.h, rtE)
                        e.reciprocal(out=stat_sb[:, 9:10], in_=stat_sb[:, 8:9])
                        e.tensor_scalar(out=rc2_sb[:, g:g + 1], in0=stat_sb[:, 8:9],
                                        scalar1=wsc_sb[:, 3:4], scalar2=1.0 / 127.0,
                                        op0=ALU.mult, op1=ALU.mult)
                        e.drain()
                        m = e.tensor_scalar(out=s127_sb[:, g:g + 1],
                                            in0=stat_sb[:, 9:10],
                                            scalar1=127.0, scalar2=None,
                                            op0=ALU.mult)
                        m.then_inc(s_ev.h, 1)
                    P.on("vector", q4r); s_rt.n += 1; s_ev.n += 1
                def q4tb(e, s=s, w=s_ev.n):
                    e.wait_ge(s_ev.h, w)
                    for g in range(4):
                        e.transpose(pb[5][0:1, 128 * g:128 * (g + 1)],
                                    s127_sb[:, g:g + 1], ident_sb[:, :]) \
                            .then_inc(s_mm.h, 1)
                P.on("tensor", q4tb); s_mm.n += 4
                def q4rowev(e, s=s, w=s_mm.n):
                    e.wait_ge(s_mm.h, w)
                    m = e.tensor_copy(out=srow_sb[0:1, 0:512],
                                      in_=pb[5][0:1, 0:512])
                    m.then_inc(s_ev.h, 1)
                P.on("vector", q4rowev); s_ev.n += 1
                def q4bc(e, w=s_ev.n):
                    e.wait_ge(s_ev.h, w)
                    e.matmul(pb[4][:, :], onesrow_sb[:, :], srow_sb[0:1, 0:512],
                             start=True, stop=True).then_inc(s_mm.h, 1)
                P.on("tensor", q4bc); s_mm.n += 1
                MM_BC = s_mm.n
                for hh in range(32):
                    def q4q(e, s=s, hh=hh, w=MM_BC):
                        if hh == 0:
                            e.wait_ge(s_mm.h, w)
                        e.tensor_tensor(hq_i32[:, :], h_sb[:, hh, 2:514],
                                        pb[4][:, :], ALU.mult)
                        e.drain()
                        m = e.tensor_copy(out=h_sb[:, hh, 2:514], in_=hq_i32[:, :])
                        if hh == 31:
                            m.then_inc(s_ev.h, 1)
                    P.on("vector", q4q)
                s_ev.n += 1
                QEV = s_ev.n
                for g in range(4):
                    def xl1(e, s=s, g=g, w=QEV, wev=s_ev.n):
                        if g == 0:
                            e.wait_ge(s_ev.h, w)
                        else:
                            e.wait_ge(s_ev.h, wev)
                        e.dma_start(out=x1t_sb[:, :],
                                    in_=x1_hbm[512 * s + 128 * g:
                                               512 * s + 128 * (g + 1), :]) \
                            .then_inc(s_xld.h, 16)
                    P.on("gpsimd", xl1); s_xld.n += 16
                    def m42(e, s=s, g=g, w=QEV, wx=s_xld.n, wev=s_ev.n):
                        if g == 0:
                            e.wait_ge(s_ev.h, w)
                        else:
                            e.wait_ge(s_ev.h, wev)
                        for dh in range(2):
                            for ch in range(32):
                                m = e.matmul(
                                    pb[2 + dh][:, :],
                                    h_sb[:, ch, 2 + 128 * g:2 + 128 * (g + 1)],
                                    fw2t_sb[:, ch, 512 * dh:512 * (dh + 1)],
                                    start=(ch == 0), stop=(ch == 31))
                        m.then_inc(s_mm.h, 1)
                    P.on("tensor", m42); s_mm.n += 1
                    def e42(e, s=s, g=g, w=s_mm.n, wx=s_xld.n):
                        e.wait_ge(s_mm.h, w)
                        e.wait_ge(s_xld.h, wx)
                        for dh in range(2):
                            e.scalar_tensor_tensor(
                                out=y32[:, 512 * dh:512 * (dh + 1)],
                                in0=pb[2 + dh][:, :], scalar=rc2_sb[:, g:g + 1],
                                in1=x1t_sb[:, 512 * dh:512 * (dh + 1)],
                                op0=ALU.mult, op1=ALU.add)
                        e.drain()
                        m = e.reduce_sum(stat_sb[:, 8:9], y32[:, :], axis=AX.X)
                        m.then_inc(s_ev.h, 1)
                    P.on("vector", e42); s_ev.n += 1
                    def s42(e, w=s_ev.n):
                        e.wait_ge(s_ev.h, w)
                        e.activation(out=scr32[:, :], in_=y32[:, :],
                                     func=AF.Square, accum_out=stat_sb[:, 9:10]) \
                            .then_inc(s_sc.h, 1)
                    P.on("scalar", s42); s_sc.n += 1
                    rtF = s_rt.n + 1
                    def v42(e, w=s_sc.n, rtF=rtF):
                        e.wait_ge(s_sc.h, w)
                        e.tensor_scalar(out=stat_sb[:, 10:11], in0=stat_sb[:, 8:9],
                                        scalar1=1.0 / D, scalar2=None,
                                        op0=ALU.mult)
                        e.drain()
                        e.tensor_mul(stat_sb[:, 11:12], stat_sb[:, 10:11],
                                     stat_sb[:, 10:11])
                        e.drain()
                        e.scalar_tensor_tensor(out=stat_sb[:, 12:13],
                                               in0=stat_sb[:, 9:10],
                                               scalar=1.0 / D,
                                               in1=stat_sb[:, 11:12],
                                               op0=ALU.mult, op1=ALU.subtract) \
                            .then_inc(s_rt.h, 1)
                        e.wait_ge(s_rt.h, rtF)
                    P.on("vector", v42); s_rt.n += 1
                    def s43(e, w=s_rt.n):
                        e.wait_ge(s_rt.h, w)
                        e.activation(out=stat_sb[:, 13:14], in_=stat_sb[:, 12:13],
                                     func=AF.Sqrt, bias=dumpc_sb[:, 3:4],
                                     scale=1.0).then_inc(s_sc.h, 1)
                    P.on("scalar", s43); s_sc.n += 1
                    rtG = s_rt.n + 1
                    def v43(e, s=s, g=g, w=s_sc.n, rtG=rtG):
                        e.wait_ge(s_sc.h, w)
                        e.reciprocal(out=stat_sb[:, 14:15],
                                     in_=stat_sb[:, 13:14]).then_inc(s_rt.h, 1)
                        e.wait_ge(s_rt.h, rtG)
                        m = e.tensor_scalar(out=scr32[:, :], in0=y32[:, :],
                                            scalar1=stat_sb[:, 10:11],
                                            scalar2=stat_sb[:, 14:15],
                                            op0=ALU.subtract, op1=ALU.mult)
                        m.then_inc(s_ev.h, 1)
                    P.on("vector", v43); s_rt.n += 1; s_ev.n += 1
                    def w43(e, s=s, g=g, w=s_ev.n):
                        e.wait_ge(s_ev.h, w)
                        r0 = 512 * s + 128 * g
                        e.dma_start(out=y_out[r0:r0 + 128, :], in_=scr32[:, :]) \
                            .then_inc(s_fin.h, 16)
                    P.on("gpsimd", w43); s_fin.n += 16


        if phases == 1:
            wdl_all = s_dlw.n
            for t in range(NTILE):
                def fincp(e, t=t, w=wdl_all):
                    e.wait_ge(s_dlw.h, w)
                    e.dma_start(out=y_out[128 * t:128 * (t + 1), 0:512],
                                in_=zbig_sb[:, :]).then_inc(s_fin.h, 16)
                    e.dma_start(out=y_out[128 * t:128 * (t + 1), 512:1024],
                                in_=zbig_sb[:, :]).then_inc(s_fin.h, 16)
                P.on("gpsimd", fincp); s_fin.n += 32
        if phases >= 2:
            emit_p234()

        def done(e, w=s_fin.n, w2=s_dlw.n):
            e.wait_ge(s_fin.h, w)
            e.wait_ge(s_dlw.h, w2)
        P.on("gpsimd", done)

        with nc.Block() as blk:
            @blk.sync
            def _(e):
                for fn in P.q["sync"]:
                    fn(e)

            @blk.gpsimd
            def _(e):
                for fn in P.q["gpsimd"]:
                    fn(e)

            @blk.vector
            def _(e):
                for fn in P.q["vector"]:
                    fn(e)

            @blk.scalar
            def _(e):
                for fn in P.q["scalar"]:
                    fn(e)

            @blk.tensor
            def _(e):
                for fn in P.q["tensor"]:
                    fn(e)

    return nc, dbg_out


def _in_maps(inputs):
    xf = lambda a: np.ascontiguousarray(np.asarray(a), dtype=np.float32)
    x = xf(inputs["L0_attn_x"]).reshape(B, T, D)
    n1 = xf(inputs["L0_attn_norm1_scale"]); n2 = xf(inputs["L0_attn_norm2_scale"])
    Wq = xf(inputs["L0_attn_Wq"]); Wk = xf(inputs["L0_attn_Wk"])
    Wv = xf(inputs["L0_attn_Wv"]); Wg = xf(inputs["L0_attn_Wg"])
    Wo = xf(inputs["L0_attn_Wo"])
    bq = xf(inputs["L0_attn_bq"]); bk = xf(inputs["L0_attn_bk"])
    bv = xf(inputs["L0_attn_bv"]); bg = xf(inputs["L0_attn_bg"])
    bo = xf(inputs["L0_attn_bo"])
    rW = xf(inputs["L0_attn_router_W"]); rb = xf(inputs["L0_attn_router_b"])
    W1 = xf(inputs["L0_attn_W1"]); b1 = xf(inputs["L0_attn_b1"])
    W2 = xf(inputs["L0_attn_W2"]); b2 = xf(inputs["L0_attn_b2"])
    fw1 = xf(inputs["L0_ffn_w1"]); fcw = xf(inputs["L0_ffn_conv_w"])
    fcb = xf(inputs["L0_ffn_conv_b"]); fw2 = xf(inputs["L0_ffn_w2"])

    maps = []
    for c in range(NCORE):
        b, hf = c // 2, c % 2
        hsel = slice(512 * hf, 512 * (hf + 1))
        wv_aug = np.zeros((D, 520), dtype=np.float32)
        bv_aug = np.zeros((520,), dtype=np.float32)
        for h8 in range(8):
            src = 512 * hf + 64 * h8
            wv_aug[:, 65 * h8:65 * h8 + 64] = Wv[:, src:src + 64]
            bv_aug[65 * h8:65 * h8 + 64] = bv[src:src + 64]
            bv_aug[65 * h8 + 64] = 1.0
        biases = np.zeros((1, 3088), dtype=np.float32)
        biases[0, 0:512] = bq[hsel]
        biases[0, 512:1024] = bk[hsel]
        biases[0, 1024:1544] = bv_aug
        biases[0, 2056:3080] = bo
        biases[0, 3080:3088] = rb
        halosel = np.zeros((16, 2), dtype=np.float32)
        if hf == 1:
            halosel[2 * (c - 1) + 1, 0] = 1.0
        if hf == 0:
            halosel[2 * (c + 1), 1] = 1.0
        woidx = np.zeros((128, 8), dtype=np.int32)
        for ch in range(8):
            src_core = 2 * b + ch // 4
            woidx[:, ch] = (16 * (512 * src_core + 128 * (ch % 4) + np.arange(128))
                            + 8 * hf)
        m = {
            "x_b": np.ascontiguousarray(x[b]),
            "wq": np.ascontiguousarray(Wq[:, hsel]),
            "wk": np.ascontiguousarray(Wk[:, hsel]),
            "wv": wv_aug,
            "wg": np.ascontiguousarray(Wg[:, hsel]),
            "wo": Wo,
            "biases": biases,
            "bg_col": np.ascontiguousarray(bg[hsel].reshape(4, 128).T),
            "n1s": np.ascontiguousarray(n1.reshape(8, 128).T),
            "n2s": np.ascontiguousarray(n2.reshape(8, 128).T),
            "wr": rW,
            "w1e": np.ascontiguousarray(W1[c]),
            "b1e": np.ascontiguousarray(b1[c].reshape(32, 128).T),
            "w2e": np.ascontiguousarray(W2[c]),
            "b2e": b2[c].reshape(1, D),
            "fw1": fw1,
            "fcw": np.ascontiguousarray(
                fcw.reshape(HID, 3).reshape(32, 128, 3).transpose(1, 0, 2)),
            "fcb": np.ascontiguousarray(fcb.reshape(32, 128).T),
            "fw2": fw2,
            "cmask": (np.arange(8) < c).astype(np.float32).reshape(8, 1),
            "tokoff": np.full((128, 1), 4096.0 * c, dtype=np.float32),
            "halosel": halosel,
            "woidx": woidx,
            "x_my": np.ascontiguousarray(x[b, 4096 * hf:4096 * (hf + 1), :]),
            "eidx": (CAP * c + np.arange(128, dtype=np.int32)).reshape(128, 1),
        }
        maps.append(m)
    return maps


_CACHE = {}


def kernel(**inputs):
    key = ("p6",)
    if key not in _CACHE:
        _CACHE[key] = build_program(phases=6)
    nc, _ = _CACHE[key]
    maps = _in_maps(inputs)
    res = run_bass_kernel_spmd(nc, maps, core_ids=list(range(NCORE)))
    y = np.zeros((B, T, D), dtype=np.float32)
    for c in range(NCORE):
        b, hf = c // 2, c % 2
        y[b, 4096 * hf:4096 * (hf + 1), :] = res.results[c]["y"]
    return y
